# version 11
# baseline (speedup 1.0000x reference)
import numpy as np
import ml_dtypes
from contextlib import ExitStack

import concourse.bass as bass
import concourse.mybir as mybir
from concourse.bass_utils import run_bass_kernel_spmd

F32 = mybir.dt.float32
BF16 = mybir.dt.bfloat16
AF = mybir.ActivationFunctionType
ALU = mybir.AluOpType
AX = mybir.AxisListType

D = 1024
T = 4096
NG = 8
H = 8
DH = 64
DI = 512
NE = 16
TILE = 512
NT = T // TILE
SC = 512
NSC = T // SC
CH = 64
NCH = SC // CH
LN_EPS = 1e-5
NORM_EPS_SQ = 1e-18

ENGINES = ("pe", "act", "dve", "pool", "sync")


class Buf:
    __slots__ = ("name", "last_write", "readers")

    def __init__(self, name=""):
        self.name = name
        self.last_write = None
        self.readers = []


class Op:
    __slots__ = ("eng", "fn", "waits", "inc_val", "idx", "dma", "sem")

    def __init__(self, eng, fn, idx, dma=False, sem=None):
        self.eng = eng
        self.fn = fn
        self.waits = {}
        self.inc_val = None
        self.idx = idx
        self.dma = dma
        self.sem = sem


class Plan:
    def __init__(self):
        self.ops = {e: [] for e in ENGINES}
        self.count = {}
        self.floor = {}

    def _milestone(self, dep):
        if dep.dma:
            return (dep.sem, self.count[dep.sem])
        eng = dep.eng
        last = None
        for o in self.ops[eng][dep.idx:]:
            if o.dma:
                continue
            if o.inc_val is not None:
                return (o.sem, o.inc_val)
            last = o
        assert last is not None
        self.count[eng] = self.count.get(eng, 0) + 1
        last.inc_val = self.count[eng]
        last.sem = eng
        return (eng, last.inc_val)

    def add(self, eng, fn, reads=(), writes=(), dma_sem=None, force_inc=False):
        op = Op(eng, fn, len(self.ops[eng]), dma=(dma_sem is not None),
                sem=dma_sem if dma_sem is not None else None)
        deps = []
        for b in reads:
            if b.last_write is not None:
                deps.append(b.last_write)
        for b in writes:
            if b.last_write is not None:
                deps.append(b.last_write)
            deps.extend(b.readers)
        for dep in deps:
            if dep.eng == eng and not dep.dma and eng in ("pe", "sync"):
                continue
            se, v = self._milestone(dep)
            if op.waits.get(se, 0) < v:
                op.waits[se] = v
        for se, v in self.floor.items():
            if op.waits.get(se, 0) < v:
                op.waits[se] = v
        self.ops[eng].append(op)
        if op.dma:
            self.count[dma_sem] = self.count.get(dma_sem, 0) + 16
            op.inc_val = self.count[dma_sem]
        else:
            self.count[eng] = self.count.get(eng, 0) + 1
            op.inc_val = self.count[eng]
            op.sem = eng
        for b in reads:
            b.readers.append(op)
        for b in writes:
            b.last_write = op
            b.readers = []
        return op

    def barrier(self):
        for eng in ENGINES:
            if self.ops[eng]:
                se, v = self._milestone(self.ops[eng][-1])
                if self.floor.get(se, 0) < v:
                    self.floor[se] = v
        for se, v in self.count.items():
            if se not in ENGINES and self.floor.get(se, 0) < v:
                self.floor[se] = v

    def emit_engine(self, ename, eng, sems):
        waited = {}
        for op in self.ops[ename]:
            for se, val in sorted(op.waits.items()):
                if waited.get(se, 0) >= val:
                    continue
                eng.wait_ge(sems[se], val)
                waited[se] = val
            inst = op.fn(eng)
            if op.inc_val is not None:
                inst.then_inc(sems[op.sem], 16 if op.dma else 1)


def build_program(stages=("ffn0", "attn", "ffn1")):
    nc = bass.Bass("TRN2", target_bir_lowering=False)

    d_x = nc.dram_tensor("xin", [128, NG, T], F32, kind="ExternalInput").ap()
    d_wm = nc.dram_tensor("wm", [128, NG, D], F32, kind="ExternalInput").ap()
    d_wo = nc.dram_tensor("wo", [128, NG, D], F32, kind="ExternalInput").ap()
    d_w1 = nc.dram_tensor("w1", [128, NE, 512], F32, kind="ExternalInput").ap()
    d_w2 = nc.dram_tensor("w2", [128, NE, 4, 128], F32, kind="ExternalInput").ap()
    d_wq = nc.dram_tensor("wq", [128, NG, DI], F32, kind="ExternalInput").ap()
    d_wk = nc.dram_tensor("wk", [128, NG, DI], F32, kind="ExternalInput").ap()
    d_wv = nc.dram_tensor("wv", [128, NG, DI], F32, kind="ExternalInput").ap()
    d_wa = nc.dram_tensor("wa", [128, NG, DI], F32, kind="ExternalInput").ap()
    d_woa = nc.dram_tensor("woa", [128, 4, D], F32, kind="ExternalInput").ap()
    d_emb = nc.dram_tensor("emb", [128, NE], F32, kind="ExternalInput").ap()
    d_b1 = nc.dram_tensor("b1", [128, NE, 4], F32, kind="ExternalInput").ap()
    d_b2 = nc.dram_tensor("b2", [NE, 128], F32, kind="ExternalInput").ap()
    d_bm = nc.dram_tensor("bm", [128, NG], F32, kind="ExternalInput").ap()
    d_bo = nc.dram_tensor("bo", [128, NG], F32, kind="ExternalInput").ap()
    d_ba = nc.dram_tensor("ba", [128, 4], F32, kind="ExternalInput").ap()
    d_gamma = nc.dram_tensor("gamma", [128, NG], F32, kind="ExternalInput").ap()
    d_beta = nc.dram_tensor("beta", [128, NG], F32, kind="ExternalInput").ap()
    d_ident = nc.dram_tensor("ident", [128, 128], BF16, kind="ExternalInput").ap()
    d_ident32 = nc.dram_tensor("ident32", [128, 128], F32, kind="ExternalInput").ap()
    d_ones1 = nc.dram_tensor("ones1", [1, 128], BF16, kind="ExternalInput").ap()
    d_onesr = nc.dram_tensor("onesr", [1, 512], BF16, kind="ExternalInput").ap()
    d_onesc = nc.dram_tensor("onesc", [128, 1], BF16, kind="ExternalInput").ap()
    d_cols = nc.dram_tensor("cols01", [128, 2], BF16, kind="ExternalInput").ap()
    d_sel = nc.dram_tensor("sel", [NE, NE * 128], BF16, kind="ExternalInput").ap()
    d_masku = nc.dram_tensor("masku", [CH, CH], BF16, kind="ExternalInput").ap()
    d_eps = nc.dram_tensor("epsv", [1, 1], F32, kind="ExternalInput").ap()
    d_out = nc.dram_tensor("xo", [128, NG, T], F32, kind="ExternalOutput").ap()

    P = Plan()
    bufs = {}

    def B(name):
        if name not in bufs:
            bufs[name] = Buf(name)
        return bufs[name]

    with ExitStack() as root:
        E = root.enter_context

        xbf = E(nc.sbuf_tensor("xbf", [128, NG, T], BF16))
        wreg = E(nc.sbuf_tensor("wreg", [128, 32768], BF16))
        ident = E(nc.sbuf_tensor("ident_sb", [128, 128], BF16))
        ident32 = E(nc.sbuf_tensor("ident32_sb", [128, 128], F32))
        ones1 = E(nc.sbuf_tensor("ones1_sb", [1, 128], BF16))
        onesr = E(nc.sbuf_tensor("onesr_sb", [1, 512], BF16))
        onesc = E(nc.sbuf_tensor("onesc_sb", [128, 1], BF16))
        cols01 = E(nc.sbuf_tensor("cols01_sb", [128, 2], BF16))
        masku = E(nc.sbuf_tensor("masku_sb", [CH, CH], BF16))
        emb_bf = E(nc.sbuf_tensor("emb_bf", [128, NE], BF16))
        b2_bf = E(nc.sbuf_tensor("b2_bf", [NE, 128], BF16))
        beta_bf = E(nc.sbuf_tensor("beta_bf", [128, NG], BF16))
        b1c = E(nc.sbuf_tensor("b1c", [128, NE, 4], F32))
        bm = E(nc.sbuf_tensor("bm_sb", [128, NG], F32))
        bo_raw = E(nc.sbuf_tensor("bo_raw", [128, NG], F32))
        bo_half = E(nc.sbuf_tensor("bo_half", [128, NG], F32))
        ba_raw = E(nc.sbuf_tensor("ba_raw", [128, 4], F32))
        ba_neg = E(nc.sbuf_tensor("ba_neg", [128, 4], F32))
        gamma = E(nc.sbuf_tensor("gamma_sb", [128, NG], F32))
        eps_sb = E(nc.sbuf_tensor("eps_sb", [1, 1], F32))

        w1v = wreg[:, 0:8192].rearrange("p (e i) -> p e i", e=NE)
        w2v = wreg[:, 8192:16384].rearrange("p (e k o) -> p e k o", e=NE, k=4)
        wmv = wreg[:, 16384:24576].rearrange("p (g o) -> p g o", g=NG)
        wov = wreg[:, 24576:32768].rearrange("p (g o) -> p g o", g=NG)
        wqv = wreg[:, 0:4096].rearrange("p (g o) -> p g o", g=NG)
        wkv = wreg[:, 4096:8192].rearrange("p (g o) -> p g o", g=NG)
        wvv = wreg[:, 8192:12288].rearrange("p (g o) -> p g o", g=NG)
        wav = wreg[:, 12288:16384].rearrange("p (g o) -> p g o", g=NG)
        woav = wreg[:, 16384:20480].rearrange("p (g o) -> p g o", g=4)

        WREG = B("wregion")
        XBF = [[B(f"x[{g},{t}]") for t in range(NT)] for g in range(NG)]

        def sld(dst, src, name):
            P.add("sync", lambda e, d=dst, s=src: e.dma_start(out=d, in_=s),
                  writes=[B(name)], dma_sem="cst")

        for dst, src, nm in [
                (ident[:], d_ident, "cst"), (ident32[:], d_ident32, "cst"),
                (ones1[:], d_ones1, "cst"), (onesr[:], d_onesr, "cst"),
                (onesc[:], d_onesc, "cst"), (cols01[:], d_cols, "cst"),
                (masku[:], d_masku, "cst"),
                (b1c[:].rearrange("p a b -> p (a b)"), d_b1.rearrange("p a b -> p (a b)"), "cst"),
                (bm[:], d_bm, "cst"), (bo_raw[:], d_bo, "cst"),
                (ba_raw[:], d_ba, "cst"), (gamma[:], d_gamma, "cst"),
                (eps_sb[:], d_eps, "cst")]:
            sld(dst, src, nm)
        for dst, src in [(emb_bf[:], d_emb), (b2_bf[:], d_b2), (beta_bf[:], d_beta)]:
            P.add("pool", lambda e, d=dst, s=src: e.dma_start(out=d, in_=s),
                  writes=[B("cstp")], dma_sem="cstp")

        def load_ffn_weights(wbsem):
            srcs = [(w1v, d_w1), (w2v, d_w2), (wmv, d_wm), (wov, d_wo)]
            for i, (dst, src) in enumerate(srcs):
                wr = [WREG] if i == len(srcs) - 1 else [B(wbsem + f"_{i}")]
                P.add("pool", lambda e, d=dst, s=src: e.dma_start(out=d, in_=s),
                      reads=([WREG] if i == 0 else []), writes=wr, dma_sem=wbsem)

        load_ffn_weights("wb0")
        for hh in range(2):
            P.add("pool", lambda e, hh=hh: e.dma_start(
                out=xbf[:, 4 * hh:4 * (hh + 1), :], in_=d_x[:, 4 * hh:4 * (hh + 1), :]),
                writes=[XBF[g][t] for g in range(4 * hh, 4 * (hh + 1)) for t in range(NT)],
                dma_sem="xin")

        P.add("dve", lambda e: e.tensor_scalar_mul(out=bo_half[:], in0=bo_raw[:], scalar1=0.5),
              reads=[B("cst")], writes=[B("boh")])
        P.add("dve", lambda e: e.tensor_scalar_mul(out=ba_neg[:], in0=ba_raw[:], scalar1=-1.0),
              reads=[B("cst")], writes=[B("ban")])

        def emit_ffn(call):
            pfx = f"f{call}:"
            wbsem = "wb0" if call == 0 else "wb2"
            with ExitStack() as fs:
                FE = fs.enter_context
                sel = FE(nc.sbuf_tensor(f"sel_sb{call}", [NE, NE, 128], BF16))
                P.add("pool", lambda e: e.dma_start(
                    out=sel[:].rearrange("p a b -> p (a b)"), in_=d_sel),
                    writes=[B(pfx + "sel")], dma_sem=wbsem)
                xn_bf = FE(nc.sbuf_tensor(f"xn_bf{call}", [128, NG, TILE], BF16))
                h_bf = FE(nc.sbuf_tensor(f"h_bf{call}", [128, NG, TILE], BF16))
                comb = FE(nc.sbuf_tensor(f"comb{call}", [128, NG, TILE], BF16))
                hmid_bf = FE(nc.sbuf_tensor(f"hmid{call}", [128, 4, 2, TILE], BF16))
                gb_bf = FE(nc.sbuf_tensor(f"gbbf{call}", [128, 2, TILE], BF16))
                tmpe = FE(nc.sbuf_tensor(f"tmpe{call}", [128, 2, TILE], BF16))
                w_tm = FE(nc.sbuf_tensor(f"w_tm{call}", [128, 32, NE], F32))
                g_sc = FE(nc.sbuf_tensor(f"g_sc{call}", [128, 32, NE], F32))
                g_row = FE(nc.sbuf_tensor(f"g_row{call}", [128, 32], F32))
                ghat_bf = FE(nc.sbuf_tensor(f"ghat_bf{call}", [128, 32, NE], BF16))
                ghat_fm = FE(nc.sbuf_tensor(f"ghat_fm{call}", [NE, NG, TILE], BF16))
                lg_f32 = FE(nc.sbuf_tensor(f"lg_f32{call}", [NE, TILE], F32))
                rows = FE(nc.sbuf_tensor(f"rows{call}", [1, 3 * TILE], F32))
                rowsbf = FE(nc.sbuf_tensor(f"rowsbf{call}", [1, 2 * TILE], BF16))
                sq_bf = FE(nc.sbuf_tensor(f"sq{call}", [128, TILE], BF16))
                tmp = FE(nc.sbuf_tensor(f"tmp{call}", [128, TILE], F32))
                ps_w1 = FE(nc.psum_tensor(f"ps_w1_{call}", [128, 4, TILE], F32))
                ps_w2 = FE(nc.psum_tensor(f"ps_w2_{call}", [128, 2, TILE], F32))
                ps_gb = FE(nc.psum_tensor(f"ps_gb_{call}", [128, TILE], F32))
                ps_x = FE(nc.psum_tensor(f"ps_x_{call}", [128, TILE], F32))

                PSW1 = [B(pfx + f"psw1_{m}") for m in range(4)]
                PSW2 = [B(pfx + "psw2a"), B(pfx + "psw2b")]
                PSGB = B(pfx + "psgb")
                PSX = B(pfx + "psx")
                ROWS = B(pfx + "rows")
                trA = ps_w2[:, 1, :]
                trB = ps_w2[0:NE, 1, :].bitcast(BF16)

                for t in range(NT):
                    tok = slice(TILE * t, TILE * (t + 1))
                    xb = [XBF[g][t] for g in range(NG)]
                    for g in range(NG):
                        P.add("pe", lambda e, g=g, tok=tok: e.matmul(
                            ps_w1[0:1, 0, :], onesc[:], xbf[:, g, tok],
                            start=(g == 0), stop=(g == NG - 1), skip_group_check=True),
                            reads=[xb[g], B("cst")], writes=[PSW1[0]])
                    for g in range(NG):
                        P.add("act", lambda e, g=g, tok=tok: e.activation(
                            out=sq_bf[:], in_=xbf[:, g, tok], func=AF.Square),
                            reads=[xb[g]], writes=[B(pfx + "sq")])
                        P.add("pe", lambda e, g=g: e.matmul(
                            ps_w1[0:1, 1, :], onesc[:], sq_bf[:],
                            start=(g == 0), stop=(g == NG - 1), skip_group_check=True),
                            reads=[B(pfx + "sq"), B("cst")], writes=[PSW1[1]])
                    P.add("dve", lambda e: e.tensor_scalar_mul(
                        out=rows[0:1, 0:TILE], in0=ps_w1[0:1, 0, :], scalar1=1.0 / D),
                        reads=[PSW1[0]], writes=[ROWS])
                    P.add("dve", lambda e: e.tensor_scalar_mul(
                        out=rows[0:1, TILE:2 * TILE], in0=ps_w1[0:1, 1, :], scalar1=1.0 / D),
                        reads=[PSW1[1]], writes=[ROWS])
                    P.add("dve", lambda e: e.tensor_tensor(
                        out=rows[0:1, 2 * TILE:3 * TILE], in0=rows[0:1, 0:TILE],
                        in1=rows[0:1, 0:TILE], op=ALU.mult),
                        reads=[ROWS], writes=[ROWS])
                    P.add("dve", lambda e: e.tensor_tensor(
                        out=rows[0:1, TILE:2 * TILE], in0=rows[0:1, TILE:2 * TILE],
                        in1=rows[0:1, 2 * TILE:3 * TILE], op=ALU.subtract),
                        reads=[ROWS], writes=[ROWS])
                    P.add("act", lambda e: e.activation(
                        out=rows[0:1, TILE:2 * TILE], in_=rows[0:1, TILE:2 * TILE],
                        func=AF.Ln, bias=eps_sb[0:1, 0:1]),
                        reads=[ROWS], writes=[ROWS])
                    P.add("act", lambda e: e.activation(
                        out=rows[0:1, TILE:2 * TILE], in_=rows[0:1, TILE:2 * TILE],
                        func=AF.Exp, scale=-0.5),
                        reads=[ROWS], writes=[ROWS])
                    P.add("dve", lambda e: e.tensor_copy(
                        out=rowsbf[0:1, :], in_=rows[0:1, 0:2 * TILE]),
                        reads=[ROWS], writes=[ROWS])
                    P.add("pe", lambda e: e.matmul(ps_w1[:, 2, :], ones1[0:1, :],
                                                   rowsbf[0:1, 0:TILE], start=True, stop=True),
                          reads=[ROWS, B("cst")], writes=[PSW1[2]])
                    P.add("pe", lambda e: e.matmul(ps_w1[:, 3, :], ones1[0:1, :],
                                                   rowsbf[0:1, TILE:2 * TILE], start=True, stop=True),
                          reads=[ROWS, B("cst")], writes=[PSW1[3]])
                    for g in range(NG):
                        P.add("dve", lambda e, g=g, tok=tok: e.tensor_tensor(
                            out=tmp[:], in0=xbf[:, g, tok], in1=ps_w1[:, 2, :], op=ALU.subtract),
                            reads=[xb[g], PSW1[2]], writes=[B(pfx + "tmp")])
                        P.add("dve", lambda e, g=g: e.tensor_tensor(
                            out=xn_bf[:, g, :], in0=tmp[:], in1=ps_w1[:, 3, :], op=ALU.mult),
                            reads=[B(pfx + "tmp"), PSW1[3]], writes=[B(pfx + f"xn{g}")])
                    for m in range(NG):
                        pb = PSW1[m % 2]
                        for k in range(NG):
                            P.add("pe", lambda e, m=m, k=k: e.matmul(
                                ps_w1[:, m % 2, :], wmv[:, k, 128 * m:128 * (m + 1)], xn_bf[:, k, :],
                                start=(k == 0), stop=(k == NG - 1), skip_group_check=True),
                                reads=[WREG, B(pfx + f"xn{k}")], writes=[pb])
                        P.add("dve", lambda e, m=m: e.tensor_scalar(
                            out=h_bf[:, m, :], in0=ps_w1[:, m % 2, :],
                            scalar1=bm[:, m:m + 1], scalar2=None, op0=ALU.add),
                            reads=[pb, B("cst")], writes=[B(pfx + f"h{m}")])
                    for g in range(NG):
                        P.add("pe", lambda e, g=g: e.matmul(ps_gb[0:NE, :], emb_bf[:], h_bf[:, g, :],
                                                            start=True, stop=True),
                              reads=[B("cstp"), B(pfx + f"h{g}")], writes=[PSGB])
                        P.add("dve", lambda e: e.tensor_copy(out=lg_f32[:], in_=ps_gb[0:NE, :]),
                              reads=[PSGB], writes=[B(pfx + "lgf")])
                        for bq in range(4):
                            P.add("pe", lambda e, bq=bq: e.transpose(
                                trA[:, 0:NE], lg_f32[:, 128 * bq:128 * (bq + 1)],
                                ident32[0:NE, 0:NE]),
                                reads=[B(pfx + "lgf"), B("cst")], writes=[PSW2[1]])
                            P.add("dve", lambda e, g=g, bq=bq: e.tensor_copy(
                                out=w_tm[:, 4 * g + bq, :], in_=trA[:, 0:NE]),
                                reads=[PSW2[1]], writes=[B(pfx + "wtm")])
                    WTM, GROW, GSC = B(pfx + "wtm"), B(pfx + "grow"), B(pfx + "gsc")
                    P.add("dve", lambda e: e.reduce_max(out=g_row[:], in_=w_tm[:], axis=AX.X),
                          reads=[WTM], writes=[GROW])
                    P.add("dve", lambda e: e.tensor_tensor(
                        out=w_tm[:], in0=w_tm[:],
                        in1=g_row[:, :, None].broadcast_to([128, 32, NE]), op=ALU.subtract),
                        reads=[WTM, GROW], writes=[WTM])
                    P.add("act", lambda e: e.activation(
                        out=w_tm[:].rearrange("p a b -> p (a b)"),
                        in_=w_tm[:].rearrange("p a b -> p (a b)"), func=AF.Exp),
                        reads=[WTM], writes=[WTM])
                    P.add("dve", lambda e: e.reduce_sum(out=g_row[:], in_=w_tm[:], axis=AX.X),
                          reads=[WTM], writes=[GROW])
                    P.add("dve", lambda e: e.reciprocal(out=g_row[:], in_=g_row[:]),
                          reads=[GROW], writes=[GROW])
                    P.add("dve", lambda e: e.tensor_tensor(
                        out=w_tm[:], in0=w_tm[:],
                        in1=g_row[:, :, None].broadcast_to([128, 32, NE]), op=ALU.mult),
                        reads=[WTM, GROW], writes=[WTM])
                    P.add("dve", lambda e: e.reduce_max(out=g_row[:], in_=w_tm[:], axis=AX.X),
                          reads=[WTM], writes=[GROW])
                    P.add("dve", lambda e: e.tensor_tensor(
                        out=g_sc[:], in0=w_tm[:],
                        in1=g_row[:, :, None].broadcast_to([128, 32, NE]), op=ALU.is_ge),
                        reads=[WTM, GROW], writes=[GSC])
                    P.add("dve", lambda e: e.scalar_tensor_tensor(
                        out=g_sc[:], in0=g_sc[:], scalar=-2.0, in1=w_tm[:],
                        op0=ALU.mult, op1=ALU.add),
                        reads=[GSC, WTM], writes=[GSC])
                    P.add("dve", lambda e: e.reduce_max(out=g_row[:], in_=g_sc[:], axis=AX.X),
                          reads=[GSC], writes=[GROW])
                    P.add("dve", lambda e: e.tensor_tensor(
                        out=g_sc[:], in0=w_tm[:],
                        in1=g_row[:, :, None].broadcast_to([128, 32, NE]), op=ALU.is_ge),
                        reads=[WTM, GROW], writes=[GSC])
                    P.add("dve", lambda e: e.tensor_tensor(
                        out=ghat_bf[:], in0=w_tm[:], in1=g_sc[:], op=ALU.mult),
                        reads=[WTM, GSC], writes=[B(pfx + "ghbf")])
                    for g in range(NG):
                        for bq in range(4):
                            P.add("pe", lambda e, g=g, bq=bq: e.transpose(
                                trB[:, 0:128], ghat_bf[:, 4 * g + bq, :], ident[:]),
                                reads=[B(pfx + "ghbf"), B("cst")], writes=[PSW2[1]])
                            P.add("dve", lambda e, g=g, bq=bq: e.tensor_copy(
                                out=ghat_fm[:, g, 128 * bq:128 * (bq + 1)], in_=trB[:, 0:128]),
                                reads=[PSW2[1]], writes=[B(pfx + f"ghfm{g}")])
                        P.add("pe", lambda e, g=g: e.matmul(ps_x[:], b2_bf[:], ghat_fm[:, g, :],
                                                            start=True, stop=True),
                              reads=[B("cstp"), B(pfx + f"ghfm{g}")], writes=[PSX])
                        P.add("dve", lambda e, g=g: e.tensor_copy(out=comb[:, g, :], in_=ps_x[:]),
                              reads=[PSX], writes=[B(pfx + f"comb{g}")])
                    dus = [(ex, 2 * gp) for ex in range(NE) for gp in range(NG // 2)]

                    def w1_phase(duu, ph):
                        exu, g0u = dus[duu]
                        for j in range(4):
                            m = 2 * ph + j // 2
                            gg = g0u + (j % 2)
                            P.add("pe", lambda e, ex=exu, g=gg, m=m, j=j: e.matmul(
                                ps_w1[:, j, :], w1v[:, ex, 128 * m:128 * (m + 1)], h_bf[:, g, :],
                                start=True, stop=True),
                                reads=[WREG, B(pfx + f"h{gg}")], writes=[PSW1[j]])

                    w1_phase(0, 0)
                    for du, (ex, g0) in enumerate(dus):
                        for h2 in range(2):
                            P.add("act", lambda e, ex=ex, h2=h2: e.activation(
                                out=hmid_bf[:, h2, :, :], in_=ps_w1[:, 2 * h2:2 * h2 + 2, :],
                                func=AF.Gelu, bias=b1c[:, ex, h2:h2 + 1]),
                                reads=[PSW1[2 * h2], PSW1[2 * h2 + 1], B("cst")],
                                writes=[B(pfx + f"hm0_{h2}"), B(pfx + f"hm1_{h2}")])
                        w1_phase(du, 1)
                        for h2 in range(2):
                            P.add("act", lambda e, ex=ex, h2=h2: e.activation(
                                out=hmid_bf[:, 2 + h2, :, :], in_=ps_w1[:, 2 * h2:2 * h2 + 2, :],
                                func=AF.Gelu, bias=b1c[:, ex, 2 + h2:3 + h2]),
                                reads=[PSW1[2 * h2], PSW1[2 * h2 + 1], B("cst")],
                                writes=[B(pfx + f"hm0_{2 + h2}"), B(pfx + f"hm1_{2 + h2}")])
                        for par in range(2):
                            g = g0 + par
                            for k in range(4):
                                P.add("pe", lambda e, ex=ex, k=k, par=par: e.matmul(
                                    ps_w2[:, par, :], w2v[:, ex, k, :], hmid_bf[:, k, par, :],
                                    start=(k == 0), stop=(k == 3), skip_group_check=True),
                                    reads=[WREG, B(pfx + f"hm{par}_{k}")], writes=[PSW2[par]])
                            P.add("pe", lambda e, ex=ex, g=g, sel=sel: e.matmul(
                                ps_gb[:], sel[:, ex, :], ghat_fm[:, g, :], start=True, stop=True),
                                reads=[B(pfx + "sel"), B(pfx + f"ghfm{g}")], writes=[PSGB])
                            P.add("dve", lambda e, par=par: e.tensor_copy(
                                out=gb_bf[:, par, :], in_=ps_gb[:]),
                                reads=[PSGB], writes=[B(pfx + f"gb{par}")])
                        if du + 1 < len(dus):
                            w1_phase(du + 1, 0)
                        for par in range(2):
                            g = g0 + par
                            P.add("dve", lambda e, par=par: e.tensor_tensor(
                                out=tmpe[:, par, :], in0=ps_w2[:, par, :], in1=gb_bf[:, par, :],
                                op=ALU.mult),
                                reads=[PSW2[par], B(pfx + f"gb{par}")], writes=[B(pfx + f"tmpe{par}")])
                            P.add("pool" if (ex % 2 == 0) else "dve",
                                  lambda e, g=g, par=par: e.tensor_tensor(
                                      out=comb[:, g, :], in0=comb[:, g, :], in1=tmpe[:, par, :],
                                      op=ALU.add),
                                  reads=[B(pfx + f"tmpe{par}"), B(pfx + f"comb{g}")],
                                  writes=[B(pfx + f"comb{g}")])
                    for m in range(NG):
                        pb = PSW1[2 + (m % 2)]
                        for k in range(NG):
                            P.add("pe", lambda e, m=m, k=k: e.matmul(
                                ps_w1[:, 2 + (m % 2), :], wov[:, k, 128 * m:128 * (m + 1)],
                                comb[:, k, :],
                                start=(k == 0), stop=(k == NG - 1), skip_group_check=True),
                                reads=[WREG, B(pfx + f"comb{k}")], writes=[pb])
                        P.add("dve", lambda e, m=m: e.tensor_scalar(
                            out=tmp[:], in0=ps_w1[:, 2 + (m % 2), :], scalar1=0.5,
                            scalar2=bo_half[:, m:m + 1], op0=ALU.mult, op1=ALU.add),
                            reads=[pb, B("boh")], writes=[B(pfx + "tmp")])
                        P.add("dve", lambda e, m=m, tok=tok: e.tensor_tensor(
                            out=xbf[:, m, tok], in0=xbf[:, m, tok], in1=tmp[:], op=ALU.add),
                            reads=[B(pfx + "tmp"), xb[m]], writes=[xb[m]])

        def emit_attn():
            with ExitStack() as asx:
                AE = asx.enter_context
                tmp_bf = AE(nc.sbuf_tensor("atmp_bf", [128, NG, SC], BF16))
                xn_bf = AE(nc.sbuf_tensor("axn_bf", [128, NG, SC], BF16))
                q_bf = AE(nc.sbuf_tensor("q_bf", [128, 4, SC], BF16))
                k_bf = AE(nc.sbuf_tensor("k_bf", [128, 4, SC], BF16))
                v_tm = AE(nc.sbuf_tensor("v_tm", [64, NCH, DI], BF16))
                sp_sb = AE(nc.sbuf_tensor("sp_sb", [128, 4, SC], F32))
                p_sb = AE(nc.sbuf_tensor("p_sb", [128, 4, CH], F32))
                lam = AE(nc.sbuf_tensor("lam", [128, 4, CH], F32))
                lami = AE(nc.sbuf_tensor("lami", [128, 4, CH], F32))
                dend = AE(nc.sbuf_tensor("dend", [128, 4, CH], F32))
                lamend = AE(nc.sbuf_tensor("lamend", [128, 4], F32))
                qt_bf = AE(nc.sbuf_tensor("qt_bf", [128, 4, CH], BF16))
                kt_bf = AE(nc.sbuf_tensor("kt_bf", [128, 4, CH], BF16))
                kh_bf = AE(nc.sbuf_tensor("kh_bf", [128, 4, CH], BF16))
                khtm = AE(nc.sbuf_tensor("khtm", [CH, 4, 128], BF16))
                atm = AE(nc.sbuf_tensor("atm", [CH, H, CH], BF16))
                oT = AE(nc.sbuf_tensor("oT", [128, 4, SC], BF16))
                s2_f32 = AE(nc.sbuf_tensor("s2_f32", [128, 4, 128], F32))
                s2_bf = AE(nc.sbuf_tensor("s2_bf", [128, 4, 128], BF16))
                r1bc = AE(nc.sbuf_tensor("r1bc", [128, SC], BF16))
                wb_sb = AE(nc.sbuf_tensor("wb_sb", [1, 4, 512], BF16))
                rows = AE(nc.sbuf_tensor("arows", [1, 4 * SC], F32))
                rowsbf = AE(nc.sbuf_tensor("arowsbf", [1, 3 * SC], BF16))
                sq_bf = AE(nc.sbuf_tensor("asq", [128, SC], BF16))
                tmp = AE(nc.sbuf_tensor("atmp", [128, SC], F32))
                tmpb = AE(nc.sbuf_tensor("atmpb", [128, SC], BF16))
                negone = AE(nc.sbuf_tensor("negone", [128, CH], F32))
                ps_a = AE(nc.psum_tensor("ps_a", [128, 2, SC], F32))
                ps_b = AE(nc.psum_tensor("ps_b", [128, 2, SC], F32))
                ps_nrm = AE(nc.psum_tensor("ps_nrm", [1, 512], F32))
                ps_at = AE(nc.psum_tensor("ps_at", [CH, H, CH], F32))
                ps_o = AE(nc.psum_tensor("ps_o", [128, 4, CH], F32))
                ps_s = AE(nc.psum_tensor("ps_s", [128, 4, 128], F32))

                PSA = [B("a:psa0"), B("a:psa1")]
                PSB = [B("a:psb0"), B("a:psb1")]
                PSNRM = B("a:psnrm")
                PSAT = B("a:psat")
                PSO = B("a:pso")
                PSS = B("a:pss")
                ROWS = B("a:rows")

                asrcs = [(wqv, d_wq), (wkv, d_wk), (wvv, d_wv),
                         (wav, d_wa), (woav, d_woa)]
                for i, (dst, src) in enumerate(asrcs):
                    wr = [WREG] if i == len(asrcs) - 1 else [B(f"wb1_{i}")]
                    P.add("pool", lambda e, d=dst, s=src: e.dma_start(out=d, in_=s),
                          writes=wr, dma_sem="wb1")

                P.add("dve", lambda e: e.memset(negone[:], -1.0), writes=[B("a:neg")])
                P.add("dve", lambda e: e.memset(s2_f32[:].rearrange("p a b -> p (a b)"), 0.0),
                      writes=[B("a:s2f")])
                P.add("dve", lambda e: e.memset(s2_bf[:].rearrange("p a b -> p (a b)"), 0.0),
                      writes=[B("a:s2b")])

                for j, wv in enumerate([wqv, wkv, wvv, wav]):
                    for k in range(NG):
                        P.add("pe", lambda e, j=j, k=k, wv=wv: e.matmul(
                            ps_nrm[0:1, :], beta_bf[:, k:k + 1], wv[:, k, :],
                            start=(k == 0), stop=(k == NG - 1)),
                            reads=[WREG, B("cstp")], writes=[PSNRM])
                    P.add("dve", lambda e, j=j: e.tensor_copy(
                        out=wb_sb[0:1, j, :], in_=ps_nrm[0:1, :]),
                        reads=[PSNRM], writes=[B("a:wb")])

                for sc in range(NSC):
                    tok = slice(SC * sc, SC * (sc + 1))
                    tt = sc
                    xb = [XBF[g][tt] for g in range(NG)]
                    for g in range(NG):
                        P.add("pe", lambda e, g=g, tok=tok: e.matmul(
                            ps_a[0:1, 0, :], onesc[:], xbf[:, g, tok],
                            start=(g == 0), stop=(g == NG - 1), skip_group_check=True),
                            reads=[xb[g], B("cst")], writes=[PSA[0]])
                    for g in range(NG):
                        P.add("act", lambda e, g=g, tok=tok: e.activation(
                            out=sq_bf[:], in_=xbf[:, g, tok], func=AF.Square),
                            reads=[xb[g]], writes=[B("a:sq")])
                        P.add("pe", lambda e, g=g: e.matmul(
                            ps_a[0:1, 1, :], onesc[:], sq_bf[:],
                            start=(g == 0), stop=(g == NG - 1), skip_group_check=True),
                            reads=[B("a:sq"), B("cst")], writes=[PSA[1]])
                    R = rows
                    P.add("dve", lambda e: e.tensor_scalar_mul(
                        out=R[0:1, 0:SC], in0=ps_a[0:1, 0, :], scalar1=1.0 / D),
                        reads=[PSA[0]], writes=[ROWS])
                    P.add("dve", lambda e: e.tensor_scalar_mul(
                        out=R[0:1, SC:2 * SC], in0=ps_a[0:1, 1, :], scalar1=1.0 / D),
                        reads=[PSA[1]], writes=[ROWS])
                    P.add("dve", lambda e: e.tensor_tensor(
                        out=R[0:1, 2 * SC:3 * SC], in0=R[0:1, 0:SC], in1=R[0:1, 0:SC],
                        op=ALU.mult),
                        reads=[ROWS], writes=[ROWS])
                    P.add("dve", lambda e: e.tensor_tensor(
                        out=R[0:1, SC:2 * SC], in0=R[0:1, SC:2 * SC],
                        in1=R[0:1, 2 * SC:3 * SC], op=ALU.subtract),
                        reads=[ROWS], writes=[ROWS])
                    P.add("act", lambda e: e.activation(
                        out=R[0:1, 2 * SC:3 * SC], in_=R[0:1, SC:2 * SC], func=AF.Ln,
                        bias=eps_sb[0:1, 0:1]),
                        reads=[ROWS], writes=[ROWS])
                    P.add("act", lambda e: e.activation(
                        out=R[0:1, 3 * SC:4 * SC], in_=R[0:1, 2 * SC:3 * SC], func=AF.Exp,
                        scale=-1.0),
                        reads=[ROWS], writes=[ROWS])
                    P.add("dve", lambda e: e.tensor_tensor(
                        out=R[0:1, SC:2 * SC], in0=R[0:1, SC:2 * SC],
                        in1=R[0:1, 3 * SC:4 * SC], op=ALU.mult),
                        reads=[ROWS], writes=[ROWS])
                    P.add("act", lambda e: e.activation(
                        out=R[0:1, 2 * SC:3 * SC], in_=R[0:1, 2 * SC:3 * SC], func=AF.Exp,
                        scale=-0.5),
                        reads=[ROWS], writes=[ROWS])
                    P.add("act", lambda e: e.activation(
                        out=R[0:1, SC:2 * SC], in_=R[0:1, SC:2 * SC], func=AF.Ln,
                        bias=eps_sb[0:1, 0:1]),
                        reads=[ROWS], writes=[ROWS])
                    P.add("act", lambda e: e.activation(
                        out=R[0:1, SC:2 * SC], in_=R[0:1, SC:2 * SC], func=AF.Exp,
                        scale=-0.5),
                        reads=[ROWS], writes=[ROWS])
                    P.add("dve", lambda e: e.tensor_tensor(
                        out=R[0:1, 3 * SC:4 * SC], in0=R[0:1, SC:2 * SC],
                        in1=R[0:1, 2 * SC:3 * SC], op=ALU.mult),
                        reads=[ROWS], writes=[ROWS])
                    P.add("dve", lambda e: e.tensor_copy(
                        out=rowsbf[0:1, :], in_=R[0:1, 0:3 * SC]),
                        reads=[ROWS], writes=[ROWS])
                    P.add("dve", lambda e: e.tensor_copy(
                        out=rowsbf[0:1, SC:2 * SC], in_=R[0:1, 3 * SC:4 * SC]),
                        reads=[ROWS], writes=[ROWS])
                    P.add("pe", lambda e: e.matmul(ps_a[:, 0, :], ones1[0:1, :],
                                                   rowsbf[0:1, 0:SC], start=True, stop=True),
                          reads=[ROWS, B("cst")], writes=[PSA[0]])
                    P.add("pe", lambda e: e.matmul(ps_a[:, 1, :], ones1[0:1, :],
                                                   rowsbf[0:1, SC:2 * SC], start=True, stop=True),
                          reads=[ROWS, B("cst")], writes=[PSA[1]])
                    P.add("pe", lambda e: e.matmul(ps_b[:, 0, :], ones1[0:1, :],
                                                   rowsbf[0:1, 2 * SC:3 * SC], start=True, stop=True),
                          reads=[ROWS, B("cst")], writes=[PSB[0]])
                    P.add("dve", lambda e: e.tensor_copy(out=r1bc[:], in_=ps_b[:, 0, :]),
                          reads=[PSB[0]], writes=[B("a:r1bc")])
                    for g in range(NG):
                        P.add("dve", lambda e, g=g, tok=tok: e.tensor_tensor(
                            out=tmp_bf[:, g, :], in0=xbf[:, g, tok], in1=ps_a[:, 0, :],
                            op=ALU.subtract),
                            reads=[xb[g], PSA[0]], writes=[B(f"a:tm{g}")])
                        P.add("dve", lambda e, g=g: e.scalar_tensor_tensor(
                            out=xn_bf[:, g, :], in0=tmp_bf[:, g, :], scalar=gamma[:, g:g + 1],
                            in1=ps_a[:, 1, :], op0=ALU.mult, op1=ALU.mult),
                            reads=[B(f"a:tm{g}"), PSA[1], B("cst")], writes=[B(f"a:xn{g}")])
                    def proj_fm(wv, wbj, m, pst, psb):
                        for k in range(NG):
                            P.add("pe", lambda e, m=m, k=k, pst=pst, wv=wv: e.matmul(
                                pst, wv[:, k, 128 * m:128 * (m + 1)], xn_bf[:, k, :],
                                start=(k == 0), stop=False, skip_group_check=True),
                                reads=[WREG, B(f"a:xn{k}")], writes=[psb])
                        P.add("pe", lambda e, m=m, pst=pst, wbj=wbj: e.matmul(
                            pst, wb_sb[0:1, wbj, 128 * m:128 * (m + 1)], onesr[0:1, :],
                            start=False, stop=True, skip_group_check=True),
                            reads=[B("a:wb"), B("cst")], writes=[psb])

                    for m in range(4):
                        pst, psb = ps_a[:, m % 2, :], PSA[m % 2]
                        proj_fm(wqv, 0, m, pst, psb)
                        P.add("dve", lambda e, m=m, pst=pst: e.tensor_copy(
                            out=q_bf[:, m, :], in_=pst),
                            reads=[psb], writes=[B(f"a:q{m}")])
                    for m in range(4):
                        pst, psb = ps_a[:, m % 2, :], PSA[m % 2]
                        proj_fm(wkv, 1, m, pst, psb)
                        P.add("dve", lambda e, m=m, pst=pst: e.tensor_copy(
                            out=k_bf[:, m, :], in_=pst),
                            reads=[psb], writes=[B(f"a:k{m}")])
                        P.add("act", lambda e, m=m: e.activation(
                            out=sq_bf[:], in_=k_bf[:, m, :], func=AF.Square),
                            reads=[B(f"a:k{m}")], writes=[B("a:sq")])
                        for half in range(2):
                            P.add("pe", lambda e, half=half: e.matmul(
                                ps_nrm[0:1, :], cols01[:, half:half + 1], sq_bf[:],
                                start=True, stop=True),
                                reads=[B("a:sq"), B("cst")], writes=[PSNRM])
                            P.add("dve", lambda e: e.tensor_scalar_max(
                                out=rows[0:1, 0:512], in0=ps_nrm[0:1, :],
                                scalar1=NORM_EPS_SQ),
                                reads=[PSNRM], writes=[ROWS])
                            P.add("act", lambda e: e.activation(
                                out=rows[0:1, 0:512], in_=rows[0:1, 0:512], func=AF.Ln),
                                reads=[ROWS], writes=[ROWS])
                            P.add("act", lambda e: e.activation(
                                out=rowsbf[0:1, 0:512], in_=rows[0:1, 0:512],
                                func=AF.Exp, scale=-0.5),
                                reads=[ROWS], writes=[ROWS])
                            P.add("pe", lambda e, half=half: e.matmul(
                                ps_b[64 * half:64 * (half + 1), 0, :], ones1[0:1, 0:64],
                                rowsbf[0:1, 0:512], start=True, stop=True),
                                reads=[ROWS, B("cst")], writes=[PSB[0]])
                        P.add("dve", lambda e, m=m: e.tensor_tensor(
                            out=k_bf[:, m, :], in0=k_bf[:, m, :], in1=ps_b[:, 0, :],
                            op=ALU.mult),
                            reads=[B(f"a:k{m}"), PSB[0]], writes=[B(f"a:k{m}")])
                    for m in range(4):
                        pst, psb = ps_a[:, m % 2, :], PSA[m % 2]
                        proj_fm(wav, 3, m, pst, psb)
                        P.add("act", lambda e, m=m, pst=pst: e.activation(
                            out=tmp[:], in_=pst, func=AF.Exp, scale=-1.0,
                            bias=ba_neg[:, m:m + 1]),
                            reads=[psb, B("ban")], writes=[B("a:tmp")])
                        P.add("act", lambda e, m=m: e.activation(
                            out=sp_sb[:, m, :], in_=tmp[:], func=AF.Ln, bias=1.0),
                            reads=[B("a:tmp")], writes=[B(f"a:sp{m}")])
                    for vc in range(NCH):
                        pst, psb = ps_b[0:64, vc % 2, :], PSB[vc % 2]
                        for k in range(NG):
                            P.add("pe", lambda e, vc=vc, k=k, pst=pst: e.matmul(
                                pst, xn_bf[:, k, 64 * vc:64 * (vc + 1)], wvv[:, k, :],
                                start=(k == 0), stop=False, skip_group_check=True),
                                reads=[WREG, B(f"a:xn{k}")], writes=[psb])
                        P.add("pe", lambda e, vc=vc, pst=pst: e.matmul(
                            pst, ones1[0:1, 0:64], wb_sb[0:1, 2, :],
                            start=False, stop=True, skip_group_check=True),
                            reads=[B("a:wb"), B("cst")], writes=[psb])
                        P.add("dve", lambda e, vc=vc, pst=pst: e.tensor_copy(
                            out=v_tm[0:64, vc, :], in_=pst),
                            reads=[psb], writes=[B(f"a:v{vc}")])
                    for c in range(NCH):
                        cs = slice(CH * c, CH * (c + 1))
                        for m in range(4):
                            P.add("dve", lambda e, m=m, cs=cs: e.tensor_tensor_scan(
                                out=p_sb[:, m, :], data0=sp_sb[:, m, cs], data1=negone[:],
                                initial=0.0, op0=ALU.subtract, op1=ALU.mult),
                                reads=[B(f"a:sp{m}"), B("a:neg")], writes=[B("g:p")])
                        P.add("act", lambda e: e.activation(
                            out=lam[:].rearrange("p a b -> p (a b)"),
                            in_=p_sb[:].rearrange("p a b -> p (a b)"), func=AF.Exp),
                            reads=[B("g:p")], writes=[B("g:lam")])
                        P.add("act", lambda e: e.activation(
                            out=lami[:].rearrange("p a b -> p (a b)"),
                            in_=p_sb[:].rearrange("p a b -> p (a b)"), func=AF.Exp, scale=-1.0),
                            reads=[B("g:p")], writes=[B("g:lami")])
                        P.add("act", lambda e: e.activation(
                            out=lamend[:], in_=p_sb[:, :, CH - 1], func=AF.Exp),
                            reads=[B("g:p")], writes=[B("g:lamend")])
                        for m in range(4):
                            P.add("dve", lambda e, m=m: e.tensor_scalar_sub(
                                out=dend[:, m, :], in0=p_sb[:, m, :],
                                scalar1=p_sb[:, m, CH - 1:CH]),
                                reads=[B("g:p")], writes=[B("g:dend")])
                        P.add("act", lambda e: e.activation(
                            out=dend[:].rearrange("p a b -> p (a b)"),
                            in_=dend[:].rearrange("p a b -> p (a b)"), func=AF.Exp, scale=-1.0),
                            reads=[B("g:dend")], writes=[B("g:dend")])
                        for m in range(4):
                            P.add("dve", lambda e, m=m, cs=cs: e.tensor_tensor(
                                out=qt_bf[:, m, :], in0=q_bf[:, m, cs], in1=lam[:, m, :],
                                op=ALU.mult),
                                reads=[B(f"a:q{m}"), B("g:lam")], writes=[B("g:qt")])
                            P.add("dve", lambda e, m=m, cs=cs: e.tensor_tensor(
                                out=kt_bf[:, m, :], in0=k_bf[:, m, cs], in1=lami[:, m, :],
                                op=ALU.mult),
                                reads=[B(f"a:k{m}"), B("g:lami")], writes=[B("g:kt")])
                            P.add("dve", lambda e, m=m, cs=cs: e.tensor_tensor(
                                out=kh_bf[:, m, :], in0=k_bf[:, m, cs], in1=dend[:, m, :],
                                op=ALU.mult),
                                reads=[B(f"a:k{m}"), B("g:dend")], writes=[B("g:kh")])
                        trK = ps_at[0:CH, :, :].rearrange("p a b -> p (a b)").bitcast(BF16)
                        for m in range(4):
                            P.add("pe", lambda e, m=m, trK=trK: e.transpose(
                                trK[:, 128 * m:128 * (m + 1)], kh_bf[:, m, :], ident[:]),
                                reads=[B("g:kh"), B("cst")], writes=[PSAT])
                            P.add("dve", lambda e, m=m, trK=trK: e.tensor_copy(
                                out=khtm[:, m, :], in_=trK[:, 128 * m:128 * (m + 1)]),
                                reads=[PSAT], writes=[B("g:khtm")])
                        for h in range(H):
                            m, r0 = h // 2, 64 * (h % 2)
                            P.add("pe", lambda e, h=h, m=m, r0=r0: e.matmul(
                                ps_at[:, h, :], kt_bf[r0:r0 + 64, m, :], qt_bf[r0:r0 + 64, m, :],
                                start=True, stop=True, skip_group_check=True),
                                reads=[B("g:kt"), B("g:qt")], writes=[PSAT])
                            P.add("dve", lambda e, h=h: e.tensor_tensor(
                                out=atm[:, h, :], in0=ps_at[:, h, :], in1=masku[:], op=ALU.mult),
                                reads=[PSAT, B("cst")], writes=[B("g:atm")])
                        for m in range(4):
                            P.add("pe", lambda e, m=m: e.matmul(
                                ps_o[:, m, :], s2_bf[:, m, :], qt_bf[:, m, :],
                                start=True, stop=False, skip_group_check=True),
                                reads=[B("a:s2b"), B("g:qt")], writes=[PSO])
                            P.add("pe", lambda e, m=m, c=c: e.matmul(
                                ps_o[0:64, m, :], v_tm[0:64, c, 128 * m:128 * m + 64],
                                atm[:, 2 * m, :],
                                start=False, stop=False, skip_group_check=True),
                                reads=[B(f"a:v{c}"), B("g:atm")], writes=[PSO])
                            P.add("pe", lambda e, m=m, c=c: e.matmul(
                                ps_o[64:128, m, :], v_tm[0:64, c, 128 * m + 64:128 * (m + 1)],
                                atm[:, 2 * m + 1, :],
                                start=False, stop=True, skip_group_check=True),
                                reads=[B(f"a:v{c}"), B("g:atm")], writes=[PSO])
                        for m in range(4):
                            for half in range(2):
                                r0 = 64 * half
                                P.add("pe", lambda e, m=m, half=half, r0=r0, c=c: e.matmul(
                                    ps_s[r0:r0 + 64, m, r0:r0 + 64],
                                    khtm[0:64, m, r0:r0 + 64],
                                    v_tm[0:64, c, 128 * m + r0:128 * m + r0 + 64],
                                    start=True, stop=True, skip_group_check=True),
                                    reads=[B("g:khtm"), B(f"a:v{c}")], writes=[PSS])
                        P.add("dve", lambda e: e.tensor_tensor(
                            out=s2_f32[:], in0=s2_f32[:],
                            in1=lamend[:, :, None].broadcast_to([128, 4, 128]),
                            op=ALU.mult),
                            reads=[B("g:lamend"), B("a:s2f")], writes=[B("a:s2f")])
                        for half in range(2):
                            r0 = 64 * half
                            P.add("dve", lambda e, r0=r0: e.tensor_tensor(
                                out=s2_f32[r0:r0 + 64, :, r0:r0 + 64],
                                in0=s2_f32[r0:r0 + 64, :, r0:r0 + 64],
                                in1=ps_s[r0:r0 + 64, :, r0:r0 + 64], op=ALU.add),
                                reads=[PSS, B("a:s2f")], writes=[B("a:s2f")])
                        P.add("dve", lambda e: e.tensor_copy(
                            out=s2_bf[:].rearrange("p a b -> p (a b)"),
                            in_=s2_f32[:].rearrange("p a b -> p (a b)")),
                            reads=[B("a:s2f")], writes=[B("a:s2b")])
                        P.add("dve", lambda e, cs=cs: e.tensor_copy(
                            out=oT[:, :, cs], in_=ps_o[:, 0:4, :]),
                            reads=[PSO], writes=[B("a:oT")])
                    for m in range(NG):
                        pst, psb = ps_a[:, m % 2, :], PSA[m % 2]
                        for k in range(4):
                            P.add("pe", lambda e, m=m, k=k, pst=pst: e.matmul(
                                pst, woav[:, k, 128 * m:128 * (m + 1)], oT[:, k, :],
                                start=(k == 0), stop=(k == 3), skip_group_check=True),
                                reads=[WREG, B("a:oT")], writes=[psb])
                        P.add("dve", lambda e, m=m, pst=pst, tok=tok: e.tensor_tensor(
                            out=tmp[:], in0=pst, in1=xbf[:, m, tok], op=ALU.add),
                            reads=[psb, xb[m]], writes=[B("a:tmp")])
                        P.add("pool", lambda e, m=m: e.tensor_tensor(
                            out=tmpb[:], in0=tmp_bf[:, m, :], in1=r1bc[:], op=ALU.mult),
                            reads=[B(f"a:tm{m}"), B("a:r1bc")], writes=[B("a:tmpb")])
                        P.add("dve", lambda e, m=m, tok=tok: e.tensor_tensor(
                            out=xbf[:, m, tok], in0=tmp[:], in1=tmpb[:], op=ALU.add),
                            reads=[B("a:tmp"), B("a:tmpb"), xb[m]], writes=[xb[m]])

        if "ffn0" in stages:
            emit_ffn(0)
            P.barrier()
        if "attn" in stages:
            emit_attn()
            P.barrier()
        if "ffn1" in stages:
            load_ffn_weights("wb2")
            emit_ffn(1)
            P.barrier()

        with ExitStack() as osx:
            ostage = osx.enter_context(nc.sbuf_tensor("ostage", [128, 2, T], F32))
            outbufs = []
            for g in range(NG):
                par = g % 2
                stbuf = B(f"ost{par}")
                P.add("dve", lambda e, g=g, par=par: e.tensor_copy(
                    out=ostage[:, par, :], in_=xbf[:, g, :]),
                    reads=[XBF[g][t] for t in range(NT)], writes=[stbuf])
                ob = B(f"out{g}")
                outbufs.append(ob)
                P.add("sync", lambda e, g=g, par=par: e.dma_start(
                    out=d_out[:, g, :], in_=ostage[:, par, :]),
                    reads=[stbuf], writes=[ob], dma_sem=f"so{par}")
            P.add("sync", lambda e: e.nop(), reads=outbufs)

        sem_names = set(ENGINES) | {s for s in P.count if s not in ENGINES}
        for op_list in P.ops.values():
            for op in op_list:
                sem_names.update(op.waits.keys())
        with ExitStack() as es:
            sems = {name: es.enter_context(nc.semaphore(name + "_sem"))
                    for name in sorted(sem_names)}
            block = es.enter_context(nc.Block())

            @block.sync
            def _(eng):
                P.emit_engine("sync", eng, sems)

            @block.vector
            def _(eng):
                P.emit_engine("dve", eng, sems)

            @block.scalar
            def _(eng):
                P.emit_engine("act", eng, sems)

            @block.tensor
            def _(eng):
                P.emit_engine("pe", eng, sems)

            @block.gpsimd
            def _(eng):
                P.emit_engine("pool", eng, sems)

    return nc



def _prep_shared(inputs):
    f32 = np.float32
    bf = ml_dtypes.bfloat16
    A = np.ascontiguousarray
    g = {}
    g["wm"] = A(np.asarray(inputs["L0_ffn_Wm"], f32).reshape(NG, 128, D).transpose(1, 0, 2))
    g["wo"] = A(np.asarray(inputs["L0_ffn_Wo"], f32).reshape(NG, 128, D).transpose(1, 0, 2))
    g["w1"] = A(np.asarray(inputs["L0_ffn_W1"], f32).transpose(1, 0, 2))
    g["w2"] = A(np.asarray(inputs["L0_ffn_W2"], f32).reshape(NE, 4, 128, 128).transpose(2, 0, 1, 3))
    g["wq"] = A(np.asarray(inputs["L0_attn_Wq"], f32).reshape(NG, 128, DI).transpose(1, 0, 2))
    g["wk"] = A(np.asarray(inputs["L0_attn_Wk"], f32).reshape(NG, 128, DI).transpose(1, 0, 2))
    g["wv"] = A(np.asarray(inputs["L0_attn_Wv"], f32).reshape(NG, 128, DI).transpose(1, 0, 2))
    g["wa"] = A(np.asarray(inputs["L0_attn_Wa"], f32).reshape(NG, 128, DI).transpose(1, 0, 2))
    g["woa"] = A(np.asarray(inputs["L0_attn_Wo"], f32).reshape(4, 128, D).transpose(1, 0, 2))
    g["emb"] = A(np.asarray(inputs["L0_ffn_expert_emb"], f32))
    g["b1"] = A(np.asarray(inputs["L0_ffn_b1"], f32).reshape(NE, 4, 128).transpose(2, 0, 1))
    g["b2"] = A(np.asarray(inputs["L0_ffn_b2"], f32))
    g["bm"] = A(np.asarray(inputs["L0_ffn_bm"], f32).reshape(NG, 128).T)
    g["bo"] = A(np.asarray(inputs["L0_ffn_bo"], f32).reshape(NG, 128).T)
    g["ba"] = A(np.asarray(inputs["L0_attn_ba"], f32).reshape(4, 128).T)
    g["gamma"] = A(np.asarray(inputs["L0_attn_gamma"], f32).reshape(NG, 128).T)
    g["beta"] = A(np.asarray(inputs["L0_attn_beta"], f32).reshape(NG, 128).T)
    g["ident"] = np.eye(128, dtype=bf)
    g["ident32"] = np.eye(128, dtype=f32)
    g["ones1"] = np.ones((1, 128), dtype=bf)
    g["onesr"] = np.ones((1, 512), dtype=bf)
    g["onesc"] = np.ones((128, 1), dtype=bf)
    cols = np.zeros((128, 2), f32)
    cols[0:64, 0] = 1.0
    cols[64:128, 1] = 1.0
    g["cols01"] = cols.astype(bf)
    selm = np.zeros((NE, NE, 128), f32)
    for e in range(NE):
        selm[e, e, :] = 1.0
    g["sel"] = selm.reshape(NE, NE * 128).astype(bf)
    g["masku"] = np.triu(np.ones((CH, CH), f32)).astype(bf)
    g["epsv"] = np.full((1, 1), LN_EPS, f32)
    return g


def kernel(**inputs):
    nc = build_program()
    shared = _prep_shared(inputs)
    x = np.asarray(inputs["L0_attn_x"], np.float32)
    in_maps = []
    for c in range(8):
        m = dict(shared)
        m["xin"] = np.ascontiguousarray(
            x[c].T.reshape(NG, 128, T).transpose(1, 0, 2))
        in_maps.append(m)
    res = run_bass_kernel_spmd(nc, in_maps, core_ids=list(range(8)))
    outs = []
    for r in res.results:
        xo = np.asarray(r["xo"])
        outs.append(xo.transpose(1, 0, 2).reshape(D, T).T)
    return np.stack(outs, axis=0).astype(np.float32)


# version 12
# speedup vs baseline: 1.2461x; 1.2461x over previous
import numpy as np
import ml_dtypes
from contextlib import ExitStack

import concourse.bass as bass
import concourse.mybir as mybir
from concourse.bass_utils import run_bass_kernel_spmd

F32 = mybir.dt.float32
BF16 = mybir.dt.bfloat16
AF = mybir.ActivationFunctionType
ALU = mybir.AluOpType
AX = mybir.AxisListType

D = 1024
T = 4096
NG = 8
H = 8
DH = 64
DI = 512
NE = 16
TILE = 512
NT = T // TILE
SC = 512
NSC = T // SC
CH = 64
NCH = SC // CH
LN_EPS = 1e-5
NORM_EPS_SQ = 1e-18

ENGINES = ("pe", "act", "dve", "pool", "sync")


class Buf:
    __slots__ = ("name", "last_write", "readers")

    def __init__(self, name=""):
        self.name = name
        self.last_write = None
        self.readers = []


class Op:
    __slots__ = ("eng", "fn", "waits", "inc_val", "idx", "dma", "sem")

    def __init__(self, eng, fn, idx, dma=False, sem=None):
        self.eng = eng
        self.fn = fn
        self.waits = {}
        self.inc_val = None
        self.idx = idx
        self.dma = dma
        self.sem = sem


class Plan:
    def __init__(self):
        self.ops = {e: [] for e in ENGINES}
        self.count = {}
        self.floor = {}

    def _milestone(self, dep):
        if dep.dma:
            return (dep.sem, self.count[dep.sem])
        eng = dep.eng
        last = None
        for o in self.ops[eng][dep.idx:]:
            if o.dma:
                continue
            if o.inc_val is not None:
                return (o.sem, o.inc_val)
            last = o
        assert last is not None
        self.count[eng] = self.count.get(eng, 0) + 1
        last.inc_val = self.count[eng]
        last.sem = eng
        return (eng, last.inc_val)

    def add(self, eng, fn, reads=(), writes=(), dma_sem=None, force_inc=False):
        op = Op(eng, fn, len(self.ops[eng]), dma=(dma_sem is not None),
                sem=dma_sem if dma_sem is not None else None)
        deps = []
        for b in reads:
            if b.last_write is not None:
                deps.append(b.last_write)
        for b in writes:
            if b.last_write is not None:
                deps.append(b.last_write)
            deps.extend(b.readers)
        for dep in deps:
            if dep.eng == eng and not dep.dma and eng in ("pe", "sync"):
                continue
            se, v = self._milestone(dep)
            if op.waits.get(se, 0) < v:
                op.waits[se] = v
        for se, v in self.floor.items():
            if op.waits.get(se, 0) < v:
                op.waits[se] = v
        self.ops[eng].append(op)
        if op.dma:
            self.count[dma_sem] = self.count.get(dma_sem, 0) + 16
            op.inc_val = self.count[dma_sem]
        else:
            self.count[eng] = self.count.get(eng, 0) + 1
            op.inc_val = self.count[eng]
            op.sem = eng
        for b in reads:
            b.readers.append(op)
        for b in writes:
            b.last_write = op
            b.readers = []
        return op

    def barrier(self):
        for eng in ENGINES:
            if self.ops[eng]:
                se, v = self._milestone(self.ops[eng][-1])
                if self.floor.get(se, 0) < v:
                    self.floor[se] = v
        for se, v in self.count.items():
            if se not in ENGINES and self.floor.get(se, 0) < v:
                self.floor[se] = v

    def emit_engine(self, ename, eng, sems):
        waited = {}
        for op in self.ops[ename]:
            for se, val in sorted(op.waits.items()):
                if waited.get(se, 0) >= val:
                    continue
                eng.wait_ge(sems[se], val)
                waited[se] = val
            inst = op.fn(eng)
            if op.inc_val is not None:
                inst.then_inc(sems[op.sem], 16 if op.dma else 1)


def build_program(stages=("ffn0", "attn", "ffn1")):
    nc = bass.Bass("TRN2", target_bir_lowering=False)

    d_x = nc.dram_tensor("xin", [128, NG, T], F32, kind="ExternalInput").ap()
    d_wm = nc.dram_tensor("wm", [128, NG, D], F32, kind="ExternalInput").ap()
    d_wo = nc.dram_tensor("wo", [128, NG, D], F32, kind="ExternalInput").ap()
    d_w1 = nc.dram_tensor("w1", [128, NE, 512], F32, kind="ExternalInput").ap()
    d_w2 = nc.dram_tensor("w2", [128, NE, 4, 128], F32, kind="ExternalInput").ap()
    d_wq = nc.dram_tensor("wq", [128, NG, DI], F32, kind="ExternalInput").ap()
    d_wk = nc.dram_tensor("wk", [128, NG, DI], F32, kind="ExternalInput").ap()
    d_wv = nc.dram_tensor("wv", [128, NG, DI], F32, kind="ExternalInput").ap()
    d_wa = nc.dram_tensor("wa", [128, NG, DI], F32, kind="ExternalInput").ap()
    d_woa = nc.dram_tensor("woa", [128, 4, D], F32, kind="ExternalInput").ap()
    d_emb = nc.dram_tensor("emb", [128, NE], F32, kind="ExternalInput").ap()
    d_b1 = nc.dram_tensor("b1", [128, NE, 4], F32, kind="ExternalInput").ap()
    d_b2 = nc.dram_tensor("b2", [NE, 128], F32, kind="ExternalInput").ap()
    d_bm = nc.dram_tensor("bm", [128, NG], F32, kind="ExternalInput").ap()
    d_bo = nc.dram_tensor("bo", [128, NG], F32, kind="ExternalInput").ap()
    d_ba = nc.dram_tensor("ba", [128, 4], F32, kind="ExternalInput").ap()
    d_gamma = nc.dram_tensor("gamma", [128, NG], F32, kind="ExternalInput").ap()
    d_beta = nc.dram_tensor("beta", [128, NG], F32, kind="ExternalInput").ap()
    d_ident = nc.dram_tensor("ident", [128, 128], BF16, kind="ExternalInput").ap()
    d_ident32 = nc.dram_tensor("ident32", [128, 128], F32, kind="ExternalInput").ap()
    d_ones1 = nc.dram_tensor("ones1", [1, 128], BF16, kind="ExternalInput").ap()
    d_onesr = nc.dram_tensor("onesr", [1, 512], BF16, kind="ExternalInput").ap()
    d_onesc = nc.dram_tensor("onesc", [128, 1], BF16, kind="ExternalInput").ap()
    d_cols = nc.dram_tensor("cols01", [128, 2], BF16, kind="ExternalInput").ap()
    d_sel = nc.dram_tensor("sel", [NE, NE * 128], BF16, kind="ExternalInput").ap()
    d_masku = nc.dram_tensor("masku", [CH, CH], BF16, kind="ExternalInput").ap()
    d_eps = nc.dram_tensor("epsv", [1, 1], F32, kind="ExternalInput").ap()
    d_out = nc.dram_tensor("xo", [128, NG, T], F32, kind="ExternalOutput").ap()

    P = Plan()
    bufs = {}

    def B(name):
        if name not in bufs:
            bufs[name] = Buf(name)
        return bufs[name]

    with ExitStack() as root:
        E = root.enter_context

        xbf = E(nc.sbuf_tensor("xbf", [128, NG, T], BF16))
        wreg = E(nc.sbuf_tensor("wreg", [128, 32768], BF16))
        ident = E(nc.sbuf_tensor("ident_sb", [128, 128], BF16))
        ident32 = E(nc.sbuf_tensor("ident32_sb", [128, 128], F32))
        ones1 = E(nc.sbuf_tensor("ones1_sb", [1, 128], BF16))
        onesr = E(nc.sbuf_tensor("onesr_sb", [1, 512], BF16))
        onesc = E(nc.sbuf_tensor("onesc_sb", [128, 1], BF16))
        cols01 = E(nc.sbuf_tensor("cols01_sb", [128, 2], BF16))
        masku = E(nc.sbuf_tensor("masku_sb", [CH, CH], BF16))
        emb_bf = E(nc.sbuf_tensor("emb_bf", [128, NE], BF16))
        b2_bf = E(nc.sbuf_tensor("b2_bf", [NE, 128], BF16))
        beta_bf = E(nc.sbuf_tensor("beta_bf", [128, NG], BF16))
        b1c = E(nc.sbuf_tensor("b1c", [128, NE, 4], F32))
        bm = E(nc.sbuf_tensor("bm_sb", [128, NG], F32))
        bo_raw = E(nc.sbuf_tensor("bo_raw", [128, NG], F32))
        bo_half = E(nc.sbuf_tensor("bo_half", [128, NG], F32))
        ba_raw = E(nc.sbuf_tensor("ba_raw", [128, 4], F32))
        ba_neg = E(nc.sbuf_tensor("ba_neg", [128, 4], F32))
        gamma = E(nc.sbuf_tensor("gamma_sb", [128, NG], F32))
        eps_sb = E(nc.sbuf_tensor("eps_sb", [1, 1], F32))

        w1v = wreg[:, 0:8192].rearrange("p (e i) -> p e i", e=NE)
        w2v = wreg[:, 8192:16384].rearrange("p (e k o) -> p e k o", e=NE, k=4)
        wmv = wreg[:, 16384:24576].rearrange("p (g o) -> p g o", g=NG)
        wov = wreg[:, 24576:32768].rearrange("p (g o) -> p g o", g=NG)
        wqv = wreg[:, 0:4096].rearrange("p (g o) -> p g o", g=NG)
        wkv = wreg[:, 4096:8192].rearrange("p (g o) -> p g o", g=NG)
        wvv = wreg[:, 8192:12288].rearrange("p (g o) -> p g o", g=NG)
        wav = wreg[:, 12288:16384].rearrange("p (g o) -> p g o", g=NG)
        woav = wreg[:, 16384:20480].rearrange("p (g o) -> p g o", g=4)

        WREG = B("wregion")
        XBF = [[B(f"x[{g},{t}]") for t in range(NT)] for g in range(NG)]

        def sld(dst, src, name):
            P.add("sync", lambda e, d=dst, s=src: e.dma_start(out=d, in_=s),
                  writes=[B(name)], dma_sem="cst")

        for dst, src, nm in [
                (ident[:], d_ident, "cst"), (ident32[:], d_ident32, "cst"),
                (ones1[:], d_ones1, "cst"), (onesr[:], d_onesr, "cst"),
                (onesc[:], d_onesc, "cst"), (cols01[:], d_cols, "cst"),
                (masku[:], d_masku, "cst"),
                (b1c[:].rearrange("p a b -> p (a b)"), d_b1.rearrange("p a b -> p (a b)"), "cst"),
                (bm[:], d_bm, "cst"), (bo_raw[:], d_bo, "cst"),
                (ba_raw[:], d_ba, "cst"), (gamma[:], d_gamma, "cst"),
                (eps_sb[:], d_eps, "cst")]:
            sld(dst, src, nm)
        for dst, src in [(emb_bf[:], d_emb), (b2_bf[:], d_b2), (beta_bf[:], d_beta)]:
            P.add("pool", lambda e, d=dst, s=src: e.dma_start(out=d, in_=s),
                  writes=[B("cstp")], dma_sem="cstp")

        def load_ffn_weights(wbsem):
            srcs = [(w1v, d_w1), (w2v, d_w2), (wmv, d_wm), (wov, d_wo)]
            for i, (dst, src) in enumerate(srcs):
                wr = [WREG] if i == len(srcs) - 1 else [B(wbsem + f"_{i}")]
                P.add("pool", lambda e, d=dst, s=src: e.dma_start(out=d, in_=s),
                      reads=([WREG] if i == 0 else []), writes=wr, dma_sem=wbsem)

        load_ffn_weights("wb0")
        for hh in range(2):
            P.add("pool", lambda e, hh=hh: e.dma_start(
                out=xbf[:, 4 * hh:4 * (hh + 1), :], in_=d_x[:, 4 * hh:4 * (hh + 1), :]),
                writes=[XBF[g][t] for g in range(4 * hh, 4 * (hh + 1)) for t in range(NT)],
                dma_sem="xin")

        P.add("dve", lambda e: e.tensor_scalar_mul(out=bo_half[:], in0=bo_raw[:], scalar1=0.5),
              reads=[B("cst")], writes=[B("boh")])
        P.add("dve", lambda e: e.tensor_scalar_mul(out=ba_neg[:], in0=ba_raw[:], scalar1=-1.0),
              reads=[B("cst")], writes=[B("ban")])

        def emit_ffn(call):
            pfx = f"f{call}:"
            wbsem = "wb0" if call == 0 else "wb2"
            with ExitStack() as fs:
                FE = fs.enter_context
                sel = FE(nc.sbuf_tensor(f"sel_sb{call}", [NE, NE, 128], BF16))
                P.add("pool", lambda e: e.dma_start(
                    out=sel[:].rearrange("p a b -> p (a b)"), in_=d_sel),
                    writes=[B(pfx + "sel")], dma_sem=wbsem)
                xn_bf = FE(nc.sbuf_tensor(f"xn_bf{call}", [128, NG, TILE], BF16))
                h_bf = FE(nc.sbuf_tensor(f"h_bf{call}", [128, NG, TILE], BF16))
                comb = FE(nc.sbuf_tensor(f"comb{call}", [128, NG, TILE], BF16))
                hmid_bf = FE(nc.sbuf_tensor(f"hmid{call}", [128, 2, 4, TILE], BF16))
                gb_bf = FE(nc.sbuf_tensor(f"gbbf{call}", [128, 2, TILE], BF16))
                tmpe = FE(nc.sbuf_tensor(f"tmpe{call}", [128, 2, TILE], BF16))
                w_tm = FE(nc.sbuf_tensor(f"w_tm{call}", [128, 32, NE], F32))
                g_sc = FE(nc.sbuf_tensor(f"g_sc{call}", [128, 32, NE], F32))
                g_row = FE(nc.sbuf_tensor(f"g_row{call}", [128, 32], F32))
                ghat_bf = FE(nc.sbuf_tensor(f"ghat_bf{call}", [128, 32, NE], BF16))
                ghat_fm = FE(nc.sbuf_tensor(f"ghat_fm{call}", [NE, NG, TILE], BF16))
                lg_f32 = FE(nc.sbuf_tensor(f"lg_f32{call}", [NE, TILE], F32))
                rows = FE(nc.sbuf_tensor(f"rows{call}", [1, 3 * TILE], F32))
                rowsbf = FE(nc.sbuf_tensor(f"rowsbf{call}", [1, 2 * TILE], BF16))
                sq_bf = FE(nc.sbuf_tensor(f"sq{call}", [128, TILE], BF16))
                tmp = FE(nc.sbuf_tensor(f"tmp{call}", [128, TILE], F32))
                ps_w1 = FE(nc.psum_tensor(f"ps_w1_{call}", [128, 4, TILE], F32))
                ps_w2 = FE(nc.psum_tensor(f"ps_w2_{call}", [128, 2, TILE], F32))
                ps_gb = FE(nc.psum_tensor(f"ps_gb_{call}", [128, TILE], F32))
                ps_x = FE(nc.psum_tensor(f"ps_x_{call}", [128, TILE], F32))

                PSW1 = [B(pfx + f"psw1_{m}") for m in range(4)]
                PSW2 = [B(pfx + "psw2a"), B(pfx + "psw2b")]
                PSGB = B(pfx + "psgb")
                PSX = B(pfx + "psx")
                ROWS = B(pfx + "rows")
                trA = ps_w2[:, 1, :]
                trB = ps_w2[0:NE, 1, :].bitcast(BF16)

                for t in range(NT):
                    tok = slice(TILE * t, TILE * (t + 1))
                    xb = [XBF[g][t] for g in range(NG)]
                    for g in range(NG):
                        P.add("pe", lambda e, g=g, tok=tok: e.matmul(
                            ps_w1[0:1, 0, :], onesc[:], xbf[:, g, tok],
                            start=(g == 0), stop=(g == NG - 1), skip_group_check=True),
                            reads=[xb[g], B("cst")], writes=[PSW1[0]])
                    for g in range(NG):
                        P.add("act", lambda e, g=g, tok=tok: e.activation(
                            out=sq_bf[:], in_=xbf[:, g, tok], func=AF.Square),
                            reads=[xb[g]], writes=[B(pfx + "sq")])
                        P.add("pe", lambda e, g=g: e.matmul(
                            ps_w1[0:1, 1, :], onesc[:], sq_bf[:],
                            start=(g == 0), stop=(g == NG - 1), skip_group_check=True),
                            reads=[B(pfx + "sq"), B("cst")], writes=[PSW1[1]])
                    P.add("dve", lambda e: e.tensor_scalar_mul(
                        out=rows[0:1, 0:TILE], in0=ps_w1[0:1, 0, :], scalar1=1.0 / D),
                        reads=[PSW1[0]], writes=[ROWS])
                    P.add("dve", lambda e: e.tensor_scalar_mul(
                        out=rows[0:1, TILE:2 * TILE], in0=ps_w1[0:1, 1, :], scalar1=1.0 / D),
                        reads=[PSW1[1]], writes=[ROWS])
                    P.add("dve", lambda e: e.tensor_tensor(
                        out=rows[0:1, 2 * TILE:3 * TILE], in0=rows[0:1, 0:TILE],
                        in1=rows[0:1, 0:TILE], op=ALU.mult),
                        reads=[ROWS], writes=[ROWS])
                    P.add("dve", lambda e: e.tensor_tensor(
                        out=rows[0:1, TILE:2 * TILE], in0=rows[0:1, TILE:2 * TILE],
                        in1=rows[0:1, 2 * TILE:3 * TILE], op=ALU.subtract),
                        reads=[ROWS], writes=[ROWS])
                    P.add("act", lambda e: e.activation(
                        out=rows[0:1, TILE:2 * TILE], in_=rows[0:1, TILE:2 * TILE],
                        func=AF.Ln, bias=eps_sb[0:1, 0:1]),
                        reads=[ROWS], writes=[ROWS])
                    P.add("act", lambda e: e.activation(
                        out=rows[0:1, TILE:2 * TILE], in_=rows[0:1, TILE:2 * TILE],
                        func=AF.Exp, scale=-0.5),
                        reads=[ROWS], writes=[ROWS])
                    P.add("dve", lambda e: e.tensor_copy(
                        out=rowsbf[0:1, :], in_=rows[0:1, 0:2 * TILE]),
                        reads=[ROWS], writes=[ROWS])
                    P.add("pe", lambda e: e.matmul(ps_w1[:, 2, :], ones1[0:1, :],
                                                   rowsbf[0:1, 0:TILE], start=True, stop=True),
                          reads=[ROWS, B("cst")], writes=[PSW1[2]])
                    P.add("pe", lambda e: e.matmul(ps_w1[:, 3, :], ones1[0:1, :],
                                                   rowsbf[0:1, TILE:2 * TILE], start=True, stop=True),
                          reads=[ROWS, B("cst")], writes=[PSW1[3]])
                    for g in range(NG):
                        P.add("dve", lambda e, g=g, tok=tok: e.tensor_tensor(
                            out=tmp[:], in0=xbf[:, g, tok], in1=ps_w1[:, 2, :], op=ALU.subtract),
                            reads=[xb[g], PSW1[2]], writes=[B(pfx + "tmp")])
                        P.add("dve", lambda e, g=g: e.tensor_tensor(
                            out=xn_bf[:, g, :], in0=tmp[:], in1=ps_w1[:, 3, :], op=ALU.mult),
                            reads=[B(pfx + "tmp"), PSW1[3]], writes=[B(pfx + f"xn{g}")])
                    for m in range(NG):
                        pb = PSW1[m % 2]
                        for k in range(NG):
                            P.add("pe", lambda e, m=m, k=k: e.matmul(
                                ps_w1[:, m % 2, :], wmv[:, k, 128 * m:128 * (m + 1)], xn_bf[:, k, :],
                                start=(k == 0), stop=(k == NG - 1), skip_group_check=True),
                                reads=[WREG, B(pfx + f"xn{k}")], writes=[pb])
                        P.add("dve", lambda e, m=m: e.tensor_scalar(
                            out=h_bf[:, m, :], in0=ps_w1[:, m % 2, :],
                            scalar1=bm[:, m:m + 1], scalar2=None, op0=ALU.add),
                            reads=[pb, B("cst")], writes=[B(pfx + f"h{m}")])
                    for g in range(NG):
                        P.add("pe", lambda e, g=g: e.matmul(ps_gb[0:NE, :], emb_bf[:], h_bf[:, g, :],
                                                            start=True, stop=True),
                              reads=[B("cstp"), B(pfx + f"h{g}")], writes=[PSGB])
                        P.add("dve", lambda e: e.tensor_copy(out=lg_f32[:], in_=ps_gb[0:NE, :]),
                              reads=[PSGB], writes=[B(pfx + "lgf")])
                        for bq in range(4):
                            P.add("pe", lambda e, bq=bq: e.transpose(
                                trA[:, 0:NE], lg_f32[:, 128 * bq:128 * (bq + 1)],
                                ident32[0:NE, 0:NE]),
                                reads=[B(pfx + "lgf"), B("cst")], writes=[PSW2[1]])
                            P.add("dve", lambda e, g=g, bq=bq: e.tensor_copy(
                                out=w_tm[:, 4 * g + bq, :], in_=trA[:, 0:NE]),
                                reads=[PSW2[1]], writes=[B(pfx + "wtm")])
                    WTM, GROW, GSC = B(pfx + "wtm"), B(pfx + "grow"), B(pfx + "gsc")
                    P.add("dve", lambda e: e.reduce_max(out=g_row[:], in_=w_tm[:], axis=AX.X),
                          reads=[WTM], writes=[GROW])
                    P.add("dve", lambda e: e.tensor_tensor(
                        out=w_tm[:], in0=w_tm[:],
                        in1=g_row[:, :, None].broadcast_to([128, 32, NE]), op=ALU.subtract),
                        reads=[WTM, GROW], writes=[WTM])
                    P.add("act", lambda e: e.activation(
                        out=w_tm[:].rearrange("p a b -> p (a b)"),
                        in_=w_tm[:].rearrange("p a b -> p (a b)"), func=AF.Exp),
                        reads=[WTM], writes=[WTM])
                    P.add("dve", lambda e: e.reduce_sum(out=g_row[:], in_=w_tm[:], axis=AX.X),
                          reads=[WTM], writes=[GROW])
                    P.add("dve", lambda e: e.reciprocal(out=g_row[:], in_=g_row[:]),
                          reads=[GROW], writes=[GROW])
                    P.add("dve", lambda e: e.tensor_tensor(
                        out=w_tm[:], in0=w_tm[:],
                        in1=g_row[:, :, None].broadcast_to([128, 32, NE]), op=ALU.mult),
                        reads=[WTM, GROW], writes=[WTM])
                    P.add("dve", lambda e: e.reduce_max(out=g_row[:], in_=w_tm[:], axis=AX.X),
                          reads=[WTM], writes=[GROW])
                    P.add("dve", lambda e: e.tensor_tensor(
                        out=g_sc[:], in0=w_tm[:],
                        in1=g_row[:, :, None].broadcast_to([128, 32, NE]), op=ALU.is_ge),
                        reads=[WTM, GROW], writes=[GSC])
                    P.add("dve", lambda e: e.scalar_tensor_tensor(
                        out=g_sc[:], in0=g_sc[:], scalar=-2.0, in1=w_tm[:],
                        op0=ALU.mult, op1=ALU.add),
                        reads=[GSC, WTM], writes=[GSC])
                    P.add("dve", lambda e: e.reduce_max(out=g_row[:], in_=g_sc[:], axis=AX.X),
                          reads=[GSC], writes=[GROW])
                    P.add("dve", lambda e: e.tensor_tensor(
                        out=g_sc[:], in0=w_tm[:],
                        in1=g_row[:, :, None].broadcast_to([128, 32, NE]), op=ALU.is_ge),
                        reads=[WTM, GROW], writes=[GSC])
                    P.add("dve", lambda e: e.tensor_tensor(
                        out=ghat_bf[:], in0=w_tm[:], in1=g_sc[:], op=ALU.mult),
                        reads=[WTM, GSC], writes=[B(pfx + "ghbf")])
                    for g in range(NG):
                        for bq in range(4):
                            P.add("pe", lambda e, g=g, bq=bq: e.transpose(
                                trB[:, 0:128], ghat_bf[:, 4 * g + bq, :], ident[:]),
                                reads=[B(pfx + "ghbf"), B("cst")], writes=[PSW2[1]])
                            P.add("dve", lambda e, g=g, bq=bq: e.tensor_copy(
                                out=ghat_fm[:, g, 128 * bq:128 * (bq + 1)], in_=trB[:, 0:128]),
                                reads=[PSW2[1]], writes=[B(pfx + f"ghfm{g}")])
                        P.add("pe", lambda e, g=g: e.matmul(ps_x[:], b2_bf[:], ghat_fm[:, g, :],
                                                            start=True, stop=True),
                              reads=[B("cstp"), B(pfx + f"ghfm{g}")], writes=[PSX])
                        P.add("dve", lambda e, g=g: e.tensor_copy(out=comb[:, g, :], in_=ps_x[:]),
                              reads=[PSX], writes=[B(pfx + f"comb{g}")])
                    units = [(ex, g) for ex in range(NE) for g in range(NG)]

                    def w1_emit(uu):
                        exu, gu = units[uu]
                        for m in range(4):
                            P.add("pe", lambda e, ex=exu, g=gu, m=m: e.matmul(
                                ps_w1[:, m, :], w1v[:, ex, 128 * m:128 * (m + 1)], h_bf[:, g, :],
                                start=True, stop=True),
                                reads=[WREG, B(pfx + f"h{gu}")], writes=[PSW1[m]])

                    w1_emit(0)
                    for u, (ex, g) in enumerate(units):
                        par = u % 2
                        for m in range(4):
                            P.add("act", lambda e, ex=ex, m=m, par=par: e.activation(
                                out=hmid_bf[:, par, m, :], in_=ps_w1[:, m, :], func=AF.Gelu,
                                bias=b1c[:, ex, m:m + 1]),
                                reads=[PSW1[m], B("cst")], writes=[B(pfx + f"hm{par}_{m}")])
                        if u + 1 < len(units):
                            w1_emit(u + 1)
                        P.add("pe", lambda e, ex=ex, g=g, sel=sel: e.matmul(
                            ps_gb[:], sel[:, ex, :], ghat_fm[:, g, :], start=True, stop=True),
                            reads=[B(pfx + "sel"), B(pfx + f"ghfm{g}")], writes=[PSGB])
                        for k in range(4):
                            P.add("pe", lambda e, ex=ex, k=k, par=par: e.matmul(
                                ps_w2[:, par, :], w2v[:, ex, k, :], hmid_bf[:, par, k, :],
                                start=(k == 0), stop=(k == 3), skip_group_check=True),
                                reads=[WREG, B(pfx + f"hm{par}_{k}")], writes=[PSW2[par]])
                        P.add("dve", lambda e, par=par: e.tensor_copy(
                            out=gb_bf[:, par, :], in_=ps_gb[:]),
                            reads=[PSGB], writes=[B(pfx + f"gb{par}")])
                        P.add("dve", lambda e, par=par: e.tensor_tensor(
                            out=tmpe[:, par, :], in0=ps_w2[:, par, :], in1=gb_bf[:, par, :],
                            op=ALU.mult),
                            reads=[PSW2[par], B(pfx + f"gb{par}")], writes=[B(pfx + f"tmpe{par}")])
                        P.add("pool" if (ex % 2 == 0) else "dve",
                              lambda e, g=g, par=par: e.tensor_tensor(
                                  out=comb[:, g, :], in0=comb[:, g, :], in1=tmpe[:, par, :],
                                  op=ALU.add),
                              reads=[B(pfx + f"tmpe{par}"), B(pfx + f"comb{g}")],
                              writes=[B(pfx + f"comb{g}")])
                    for m in range(NG):
                        pb = PSW1[2 + (m % 2)]
                        for k in range(NG):
                            P.add("pe", lambda e, m=m, k=k: e.matmul(
                                ps_w1[:, 2 + (m % 2), :], wov[:, k, 128 * m:128 * (m + 1)],
                                comb[:, k, :],
                                start=(k == 0), stop=(k == NG - 1), skip_group_check=True),
                                reads=[WREG, B(pfx + f"comb{k}")], writes=[pb])
                        P.add("dve", lambda e, m=m: e.tensor_scalar(
                            out=tmp[:], in0=ps_w1[:, 2 + (m % 2), :], scalar1=0.5,
                            scalar2=bo_half[:, m:m + 1], op0=ALU.mult, op1=ALU.add),
                            reads=[pb, B("boh")], writes=[B(pfx + "tmp")])
                        P.add("dve", lambda e, m=m, tok=tok: e.tensor_tensor(
                            out=xbf[:, m, tok], in0=xbf[:, m, tok], in1=tmp[:], op=ALU.add),
                            reads=[B(pfx + "tmp"), xb[m]], writes=[xb[m]])

        def emit_attn():
            with ExitStack() as asx:
                AE = asx.enter_context
                tmp_bf = AE(nc.sbuf_tensor("atmp_bf", [128, NG, SC], BF16))
                xn_bf = AE(nc.sbuf_tensor("axn_bf", [128, NG, SC], BF16))
                q_bf = AE(nc.sbuf_tensor("q_bf", [128, 4, SC], BF16))
                k_bf = AE(nc.sbuf_tensor("k_bf", [128, 4, SC], BF16))
                v_tm = AE(nc.sbuf_tensor("v_tm", [64, NCH, DI], BF16))
                sp_sb = AE(nc.sbuf_tensor("sp_sb", [128, 4, SC], F32))
                p_sb = AE(nc.sbuf_tensor("p_sb", [128, 4, CH], F32))
                lam = AE(nc.sbuf_tensor("lam", [128, 4, CH], F32))
                lami = AE(nc.sbuf_tensor("lami", [128, 4, CH], F32))
                dend = AE(nc.sbuf_tensor("dend", [128, 4, CH], F32))
                lamend = AE(nc.sbuf_tensor("lamend", [128, 4], F32))
                qt_bf = AE(nc.sbuf_tensor("qt_bf", [128, 4, CH], BF16))
                kt_bf = AE(nc.sbuf_tensor("kt_bf", [128, 4, CH], BF16))
                kh_bf = AE(nc.sbuf_tensor("kh_bf", [128, 4, CH], BF16))
                khtm = AE(nc.sbuf_tensor("khtm", [CH, 4, 128], BF16))
                atm = AE(nc.sbuf_tensor("atm", [CH, H, CH], BF16))
                oT = AE(nc.sbuf_tensor("oT", [128, 4, SC], BF16))
                s2_f32 = AE(nc.sbuf_tensor("s2_f32", [128, 4, 128], F32))
                s2_bf = AE(nc.sbuf_tensor("s2_bf", [128, 4, 128], BF16))
                r1bc = AE(nc.sbuf_tensor("r1bc", [128, SC], BF16))
                wb_sb = AE(nc.sbuf_tensor("wb_sb", [1, 4, 512], BF16))
                rows = AE(nc.sbuf_tensor("arows", [1, 4 * SC], F32))
                rowsbf = AE(nc.sbuf_tensor("arowsbf", [1, 3 * SC], BF16))
                sq_bf = AE(nc.sbuf_tensor("asq", [128, SC], BF16))
                tmp = AE(nc.sbuf_tensor("atmp", [128, SC], F32))
                tmpb = AE(nc.sbuf_tensor("atmpb", [128, SC], BF16))
                negone = AE(nc.sbuf_tensor("negone", [128, CH], F32))
                ps_a = AE(nc.psum_tensor("ps_a", [128, 2, SC], F32))
                ps_b = AE(nc.psum_tensor("ps_b", [128, 2, SC], F32))
                ps_nrm = AE(nc.psum_tensor("ps_nrm", [1, 512], F32))
                ps_at = AE(nc.psum_tensor("ps_at", [CH, H, CH], F32))
                ps_o = AE(nc.psum_tensor("ps_o", [128, 4, CH], F32))
                ps_s = AE(nc.psum_tensor("ps_s", [128, 4, 128], F32))

                PSA = [B("a:psa0"), B("a:psa1")]
                PSB = [B("a:psb0"), B("a:psb1")]
                PSNRM = B("a:psnrm")
                PSAT = B("a:psat")
                PSO = B("a:pso")
                PSS = B("a:pss")
                ROWS = B("a:rows")

                asrcs = [(wqv, d_wq), (wkv, d_wk), (wvv, d_wv),
                         (wav, d_wa), (woav, d_woa)]
                for i, (dst, src) in enumerate(asrcs):
                    wr = [WREG] if i == len(asrcs) - 1 else [B(f"wb1_{i}")]
                    P.add("pool", lambda e, d=dst, s=src: e.dma_start(out=d, in_=s),
                          writes=wr, dma_sem="wb1")

                P.add("dve", lambda e: e.memset(negone[:], -1.0), writes=[B("a:neg")])
                P.add("dve", lambda e: e.memset(s2_f32[:].rearrange("p a b -> p (a b)"), 0.0),
                      writes=[B("a:s2f")])
                P.add("dve", lambda e: e.memset(s2_bf[:].rearrange("p a b -> p (a b)"), 0.0),
                      writes=[B("a:s2b")])

                for j, wv in enumerate([wqv, wkv, wvv, wav]):
                    for k in range(NG):
                        P.add("pe", lambda e, j=j, k=k, wv=wv: e.matmul(
                            ps_nrm[0:1, :], beta_bf[:, k:k + 1], wv[:, k, :],
                            start=(k == 0), stop=(k == NG - 1)),
                            reads=[WREG, B("cstp")], writes=[PSNRM])
                    P.add("dve", lambda e, j=j: e.tensor_copy(
                        out=wb_sb[0:1, j, :], in_=ps_nrm[0:1, :]),
                        reads=[PSNRM], writes=[B("a:wb")])

                for sc in range(NSC):
                    tok = slice(SC * sc, SC * (sc + 1))
                    tt = sc
                    xb = [XBF[g][tt] for g in range(NG)]
                    for g in range(NG):
                        P.add("pe", lambda e, g=g, tok=tok: e.matmul(
                            ps_a[0:1, 0, :], onesc[:], xbf[:, g, tok],
                            start=(g == 0), stop=(g == NG - 1), skip_group_check=True),
                            reads=[xb[g], B("cst")], writes=[PSA[0]])
                    for g in range(NG):
                        P.add("act", lambda e, g=g, tok=tok: e.activation(
                            out=sq_bf[:], in_=xbf[:, g, tok], func=AF.Square),
                            reads=[xb[g]], writes=[B("a:sq")])
                        P.add("pe", lambda e, g=g: e.matmul(
                            ps_a[0:1, 1, :], onesc[:], sq_bf[:],
                            start=(g == 0), stop=(g == NG - 1), skip_group_check=True),
                            reads=[B("a:sq"), B("cst")], writes=[PSA[1]])
                    R = rows
                    P.add("dve", lambda e: e.tensor_scalar_mul(
                        out=R[0:1, 0:SC], in0=ps_a[0:1, 0, :], scalar1=1.0 / D),
                        reads=[PSA[0]], writes=[ROWS])
                    P.add("dve", lambda e: e.tensor_scalar_mul(
                        out=R[0:1, SC:2 * SC], in0=ps_a[0:1, 1, :], scalar1=1.0 / D),
                        reads=[PSA[1]], writes=[ROWS])
                    P.add("dve", lambda e: e.tensor_tensor(
                        out=R[0:1, 2 * SC:3 * SC], in0=R[0:1, 0:SC], in1=R[0:1, 0:SC],
                        op=ALU.mult),
                        reads=[ROWS], writes=[ROWS])
                    P.add("dve", lambda e: e.tensor_tensor(
                        out=R[0:1, SC:2 * SC], in0=R[0:1, SC:2 * SC],
                        in1=R[0:1, 2 * SC:3 * SC], op=ALU.subtract),
                        reads=[ROWS], writes=[ROWS])
                    P.add("act", lambda e: e.activation(
                        out=R[0:1, 2 * SC:3 * SC], in_=R[0:1, SC:2 * SC], func=AF.Ln,
                        bias=eps_sb[0:1, 0:1]),
                        reads=[ROWS], writes=[ROWS])
                    P.add("act", lambda e: e.activation(
                        out=R[0:1, 3 * SC:4 * SC], in_=R[0:1, 2 * SC:3 * SC], func=AF.Exp,
                        scale=-1.0),
                        reads=[ROWS], writes=[ROWS])
                    P.add("dve", lambda e: e.tensor_tensor(
                        out=R[0:1, SC:2 * SC], in0=R[0:1, SC:2 * SC],
                        in1=R[0:1, 3 * SC:4 * SC], op=ALU.mult),
                        reads=[ROWS], writes=[ROWS])
                    P.add("act", lambda e: e.activation(
                        out=R[0:1, 2 * SC:3 * SC], in_=R[0:1, 2 * SC:3 * SC], func=AF.Exp,
                        scale=-0.5),
                        reads=[ROWS], writes=[ROWS])
                    P.add("act", lambda e: e.activation(
                        out=R[0:1, SC:2 * SC], in_=R[0:1, SC:2 * SC], func=AF.Ln,
                        bias=eps_sb[0:1, 0:1]),
                        reads=[ROWS], writes=[ROWS])
                    P.add("act", lambda e: e.activation(
                        out=R[0:1, SC:2 * SC], in_=R[0:1, SC:2 * SC], func=AF.Exp,
                        scale=-0.5),
                        reads=[ROWS], writes=[ROWS])
                    P.add("dve", lambda e: e.tensor_tensor(
                        out=R[0:1, 3 * SC:4 * SC], in0=R[0:1, SC:2 * SC],
                        in1=R[0:1, 2 * SC:3 * SC], op=ALU.mult),
                        reads=[ROWS], writes=[ROWS])
                    P.add("dve", lambda e: e.tensor_copy(
                        out=rowsbf[0:1, :], in_=R[0:1, 0:3 * SC]),
                        reads=[ROWS], writes=[ROWS])
                    P.add("dve", lambda e: e.tensor_copy(
                        out=rowsbf[0:1, SC:2 * SC], in_=R[0:1, 3 * SC:4 * SC]),
                        reads=[ROWS], writes=[ROWS])
                    P.add("pe", lambda e: e.matmul(ps_a[:, 0, :], ones1[0:1, :],
                                                   rowsbf[0:1, 0:SC], start=True, stop=True),
                          reads=[ROWS, B("cst")], writes=[PSA[0]])
                    P.add("pe", lambda e: e.matmul(ps_a[:, 1, :], ones1[0:1, :],
                                                   rowsbf[0:1, SC:2 * SC], start=True, stop=True),
                          reads=[ROWS, B("cst")], writes=[PSA[1]])
                    P.add("pe", lambda e: e.matmul(ps_b[:, 0, :], ones1[0:1, :],
                                                   rowsbf[0:1, 2 * SC:3 * SC], start=True, stop=True),
                          reads=[ROWS, B("cst")], writes=[PSB[0]])
                    P.add("dve", lambda e: e.tensor_copy(out=r1bc[:], in_=ps_b[:, 0, :]),
                          reads=[PSB[0]], writes=[B("a:r1bc")])
                    for g in range(NG):
                        P.add("dve", lambda e, g=g, tok=tok: e.tensor_tensor(
                            out=tmp_bf[:, g, :], in0=xbf[:, g, tok], in1=ps_a[:, 0, :],
                            op=ALU.subtract),
                            reads=[xb[g], PSA[0]], writes=[B(f"a:tm{g}")])
                        P.add("dve", lambda e, g=g: e.scalar_tensor_tensor(
                            out=xn_bf[:, g, :], in0=tmp_bf[:, g, :], scalar=gamma[:, g:g + 1],
                            in1=ps_a[:, 1, :], op0=ALU.mult, op1=ALU.mult),
                            reads=[B(f"a:tm{g}"), PSA[1], B("cst")], writes=[B(f"a:xn{g}")])
                    def proj_fm(wv, wbj, m, pst, psb):
                        for k in range(NG):
                            P.add("pe", lambda e, m=m, k=k, pst=pst, wv=wv: e.matmul(
                                pst, wv[:, k, 128 * m:128 * (m + 1)], xn_bf[:, k, :],
                                start=(k == 0), stop=False, skip_group_check=True),
                                reads=[WREG, B(f"a:xn{k}")], writes=[psb])
                        P.add("pe", lambda e, m=m, pst=pst, wbj=wbj: e.matmul(
                            pst, wb_sb[0:1, wbj, 128 * m:128 * (m + 1)], onesr[0:1, :],
                            start=False, stop=True, skip_group_check=True),
                            reads=[B("a:wb"), B("cst")], writes=[psb])

                    for m in range(4):
                        pst, psb = ps_a[:, m % 2, :], PSA[m % 2]
                        proj_fm(wqv, 0, m, pst, psb)
                        P.add("dve", lambda e, m=m, pst=pst: e.tensor_copy(
                            out=q_bf[:, m, :], in_=pst),
                            reads=[psb], writes=[B(f"a:q{m}")])
                    for m in range(4):
                        pst, psb = ps_a[:, m % 2, :], PSA[m % 2]
                        proj_fm(wkv, 1, m, pst, psb)
                        P.add("dve", lambda e, m=m, pst=pst: e.tensor_copy(
                            out=k_bf[:, m, :], in_=pst),
                            reads=[psb], writes=[B(f"a:k{m}")])
                        P.add("act", lambda e, m=m: e.activation(
                            out=sq_bf[:], in_=k_bf[:, m, :], func=AF.Square),
                            reads=[B(f"a:k{m}")], writes=[B("a:sq")])
                        for half in range(2):
                            P.add("pe", lambda e, half=half: e.matmul(
                                ps_nrm[0:1, :], cols01[:, half:half + 1], sq_bf[:],
                                start=True, stop=True),
                                reads=[B("a:sq"), B("cst")], writes=[PSNRM])
                            P.add("dve", lambda e: e.tensor_scalar_max(
                                out=rows[0:1, 0:512], in0=ps_nrm[0:1, :],
                                scalar1=NORM_EPS_SQ),
                                reads=[PSNRM], writes=[ROWS])
                            P.add("act", lambda e: e.activation(
                                out=rows[0:1, 0:512], in_=rows[0:1, 0:512], func=AF.Ln),
                                reads=[ROWS], writes=[ROWS])
                            P.add("act", lambda e: e.activation(
                                out=rowsbf[0:1, 0:512], in_=rows[0:1, 0:512],
                                func=AF.Exp, scale=-0.5),
                                reads=[ROWS], writes=[ROWS])
                            P.add("pe", lambda e, half=half: e.matmul(
                                ps_b[64 * half:64 * (half + 1), 0, :], ones1[0:1, 0:64],
                                rowsbf[0:1, 0:512], start=True, stop=True),
                                reads=[ROWS, B("cst")], writes=[PSB[0]])
                        P.add("dve", lambda e, m=m: e.tensor_tensor(
                            out=k_bf[:, m, :], in0=k_bf[:, m, :], in1=ps_b[:, 0, :],
                            op=ALU.mult),
                            reads=[B(f"a:k{m}"), PSB[0]], writes=[B(f"a:k{m}")])
                    for m in range(4):
                        pst, psb = ps_a[:, m % 2, :], PSA[m % 2]
                        proj_fm(wav, 3, m, pst, psb)
                        P.add("act", lambda e, m=m, pst=pst: e.activation(
                            out=tmp[:], in_=pst, func=AF.Exp, scale=-1.0,
                            bias=ba_neg[:, m:m + 1]),
                            reads=[psb, B("ban")], writes=[B("a:tmp")])
                        P.add("act", lambda e, m=m: e.activation(
                            out=sp_sb[:, m, :], in_=tmp[:], func=AF.Ln, bias=1.0),
                            reads=[B("a:tmp")], writes=[B(f"a:sp{m}")])
                    for vc in range(NCH):
                        pst, psb = ps_b[0:64, vc % 2, :], PSB[vc % 2]
                        for k in range(NG):
                            P.add("pe", lambda e, vc=vc, k=k, pst=pst: e.matmul(
                                pst, xn_bf[:, k, 64 * vc:64 * (vc + 1)], wvv[:, k, :],
                                start=(k == 0), stop=False, skip_group_check=True),
                                reads=[WREG, B(f"a:xn{k}")], writes=[psb])
                        P.add("pe", lambda e, vc=vc, pst=pst: e.matmul(
                            pst, ones1[0:1, 0:64], wb_sb[0:1, 2, :],
                            start=False, stop=True, skip_group_check=True),
                            reads=[B("a:wb"), B("cst")], writes=[psb])
                        P.add("dve", lambda e, vc=vc, pst=pst: e.tensor_copy(
                            out=v_tm[0:64, vc, :], in_=pst),
                            reads=[psb], writes=[B(f"a:v{vc}")])
                    for c in range(NCH):
                        cs = slice(CH * c, CH * (c + 1))
                        for m in range(4):
                            P.add("dve", lambda e, m=m, cs=cs: e.tensor_tensor_scan(
                                out=p_sb[:, m, :], data0=sp_sb[:, m, cs], data1=negone[:],
                                initial=0.0, op0=ALU.subtract, op1=ALU.mult),
                                reads=[B(f"a:sp{m}"), B("a:neg")], writes=[B("g:p")])
                        P.add("act", lambda e: e.activation(
                            out=lam[:].rearrange("p a b -> p (a b)"),
                            in_=p_sb[:].rearrange("p a b -> p (a b)"), func=AF.Exp),
                            reads=[B("g:p")], writes=[B("g:lam")])
                        P.add("act", lambda e: e.activation(
                            out=lami[:].rearrange("p a b -> p (a b)"),
                            in_=p_sb[:].rearrange("p a b -> p (a b)"), func=AF.Exp, scale=-1.0),
                            reads=[B("g:p")], writes=[B("g:lami")])
                        P.add("act", lambda e: e.activation(
                            out=lamend[:], in_=p_sb[:, :, CH - 1], func=AF.Exp),
                            reads=[B("g:p")], writes=[B("g:lamend")])
                        for m in range(4):
                            P.add("dve", lambda e, m=m: e.tensor_scalar_sub(
                                out=dend[:, m, :], in0=p_sb[:, m, :],
                                scalar1=p_sb[:, m, CH - 1:CH]),
                                reads=[B("g:p")], writes=[B("g:dend")])
                        P.add("act", lambda e: e.activation(
                            out=dend[:].rearrange("p a b -> p (a b)"),
                            in_=dend[:].rearrange("p a b -> p (a b)"), func=AF.Exp, scale=-1.0),
                            reads=[B("g:dend")], writes=[B("g:dend")])
                        for m in range(4):
                            P.add("dve", lambda e, m=m, cs=cs: e.tensor_tensor(
                                out=qt_bf[:, m, :], in0=q_bf[:, m, cs], in1=lam[:, m, :],
                                op=ALU.mult),
                                reads=[B(f"a:q{m}"), B("g:lam")], writes=[B("g:qt")])
                            P.add("dve", lambda e, m=m, cs=cs: e.tensor_tensor(
                                out=kt_bf[:, m, :], in0=k_bf[:, m, cs], in1=lami[:, m, :],
                                op=ALU.mult),
                                reads=[B(f"a:k{m}"), B("g:lami")], writes=[B("g:kt")])
                            P.add("dve", lambda e, m=m, cs=cs: e.tensor_tensor(
                                out=kh_bf[:, m, :], in0=k_bf[:, m, cs], in1=dend[:, m, :],
                                op=ALU.mult),
                                reads=[B(f"a:k{m}"), B("g:dend")], writes=[B("g:kh")])
                        trK = ps_at[0:CH, :, :].rearrange("p a b -> p (a b)").bitcast(BF16)
                        for m in range(4):
                            P.add("pe", lambda e, m=m, trK=trK: e.transpose(
                                trK[:, 128 * m:128 * (m + 1)], kh_bf[:, m, :], ident[:]),
                                reads=[B("g:kh"), B("cst")], writes=[PSAT])
                            P.add("dve", lambda e, m=m, trK=trK: e.tensor_copy(
                                out=khtm[:, m, :], in_=trK[:, 128 * m:128 * (m + 1)]),
                                reads=[PSAT], writes=[B("g:khtm")])
                        for h in range(H):
                            m, r0 = h // 2, 64 * (h % 2)
                            P.add("pe", lambda e, h=h, m=m, r0=r0: e.matmul(
                                ps_at[:, h, :], kt_bf[r0:r0 + 64, m, :], qt_bf[r0:r0 + 64, m, :],
                                start=True, stop=True, skip_group_check=True),
                                reads=[B("g:kt"), B("g:qt")], writes=[PSAT])
                            P.add("dve", lambda e, h=h: e.tensor_tensor(
                                out=atm[:, h, :], in0=ps_at[:, h, :], in1=masku[:], op=ALU.mult),
                                reads=[PSAT, B("cst")], writes=[B("g:atm")])
                        for m in range(4):
                            P.add("pe", lambda e, m=m: e.matmul(
                                ps_o[:, m, :], s2_bf[:, m, :], qt_bf[:, m, :],
                                start=True, stop=False, skip_group_check=True),
                                reads=[B("a:s2b"), B("g:qt")], writes=[PSO])
                            P.add("pe", lambda e, m=m, c=c: e.matmul(
                                ps_o[0:64, m, :], v_tm[0:64, c, 128 * m:128 * m + 64],
                                atm[:, 2 * m, :],
                                start=False, stop=False, skip_group_check=True),
                                reads=[B(f"a:v{c}"), B("g:atm")], writes=[PSO])
                            P.add("pe", lambda e, m=m, c=c: e.matmul(
                                ps_o[64:128, m, :], v_tm[0:64, c, 128 * m + 64:128 * (m + 1)],
                                atm[:, 2 * m + 1, :],
                                start=False, stop=True, skip_group_check=True),
                                reads=[B(f"a:v{c}"), B("g:atm")], writes=[PSO])
                        for m in range(4):
                            for half in range(2):
                                r0 = 64 * half
                                P.add("pe", lambda e, m=m, half=half, r0=r0, c=c: e.matmul(
                                    ps_s[r0:r0 + 64, m, r0:r0 + 64],
                                    khtm[0:64, m, r0:r0 + 64],
                                    v_tm[0:64, c, 128 * m + r0:128 * m + r0 + 64],
                                    start=True, stop=True, skip_group_check=True),
                                    reads=[B("g:khtm"), B(f"a:v{c}")], writes=[PSS])
                        P.add("dve", lambda e: e.tensor_tensor(
                            out=s2_f32[:], in0=s2_f32[:],
                            in1=lamend[:, :, None].broadcast_to([128, 4, 128]),
                            op=ALU.mult),
                            reads=[B("g:lamend"), B("a:s2f")], writes=[B("a:s2f")])
                        for half in range(2):
                            r0 = 64 * half
                            P.add("dve", lambda e, r0=r0: e.tensor_tensor(
                                out=s2_f32[r0:r0 + 64, :, r0:r0 + 64],
                                in0=s2_f32[r0:r0 + 64, :, r0:r0 + 64],
                                in1=ps_s[r0:r0 + 64, :, r0:r0 + 64], op=ALU.add),
                                reads=[PSS, B("a:s2f")], writes=[B("a:s2f")])
                        P.add("dve", lambda e: e.tensor_copy(
                            out=s2_bf[:].rearrange("p a b -> p (a b)"),
                            in_=s2_f32[:].rearrange("p a b -> p (a b)")),
                            reads=[B("a:s2f")], writes=[B("a:s2b")])
                        P.add("dve", lambda e, cs=cs: e.tensor_copy(
                            out=oT[:, :, cs], in_=ps_o[:, 0:4, :]),
                            reads=[PSO], writes=[B("a:oT")])
                    for m in range(NG):
                        pst, psb = ps_a[:, m % 2, :], PSA[m % 2]
                        for k in range(4):
                            P.add("pe", lambda e, m=m, k=k, pst=pst: e.matmul(
                                pst, woav[:, k, 128 * m:128 * (m + 1)], oT[:, k, :],
                                start=(k == 0), stop=(k == 3), skip_group_check=True),
                                reads=[WREG, B("a:oT")], writes=[psb])
                        P.add("dve", lambda e, m=m, pst=pst, tok=tok: e.tensor_tensor(
                            out=tmp[:], in0=pst, in1=xbf[:, m, tok], op=ALU.add),
                            reads=[psb, xb[m]], writes=[B("a:tmp")])
                        P.add("pool", lambda e, m=m: e.tensor_tensor(
                            out=tmpb[:], in0=tmp_bf[:, m, :], in1=r1bc[:], op=ALU.mult),
                            reads=[B(f"a:tm{m}"), B("a:r1bc")], writes=[B("a:tmpb")])
                        P.add("dve", lambda e, m=m, tok=tok: e.tensor_tensor(
                            out=xbf[:, m, tok], in0=tmp[:], in1=tmpb[:], op=ALU.add),
                            reads=[B("a:tmp"), B("a:tmpb"), xb[m]], writes=[xb[m]])

        if "ffn0" in stages:
            emit_ffn(0)
            P.barrier()
        if "attn" in stages:
            emit_attn()
            P.barrier()
        if "ffn1" in stages:
            load_ffn_weights("wb2")
            emit_ffn(1)
            P.barrier()

        with ExitStack() as osx:
            ostage = osx.enter_context(nc.sbuf_tensor("ostage", [128, 2, T], F32))
            outbufs = []
            for g in range(NG):
                par = g % 2
                stbuf = B(f"ost{par}")
                P.add("dve", lambda e, g=g, par=par: e.tensor_copy(
                    out=ostage[:, par, :], in_=xbf[:, g, :]),
                    reads=[XBF[g][t] for t in range(NT)], writes=[stbuf])
                ob = B(f"out{g}")
                outbufs.append(ob)
                P.add("sync", lambda e, g=g, par=par: e.dma_start(
                    out=d_out[:, g, :], in_=ostage[:, par, :]),
                    reads=[stbuf], writes=[ob], dma_sem=f"so{par}")
            P.add("sync", lambda e: e.nop(), reads=outbufs)

        sem_names = set(ENGINES) | {s for s in P.count if s not in ENGINES}
        for op_list in P.ops.values():
            for op in op_list:
                sem_names.update(op.waits.keys())
        with ExitStack() as es:
            sems = {name: es.enter_context(nc.semaphore(name + "_sem"))
                    for name in sorted(sem_names)}
            block = es.enter_context(nc.Block())

            @block.sync
            def _(eng):
                P.emit_engine("sync", eng, sems)

            @block.vector
            def _(eng):
                P.emit_engine("dve", eng, sems)

            @block.scalar
            def _(eng):
                P.emit_engine("act", eng, sems)

            @block.tensor
            def _(eng):
                P.emit_engine("pe", eng, sems)

            @block.gpsimd
            def _(eng):
                P.emit_engine("pool", eng, sems)

    return nc



def _prep_shared(inputs):
    f32 = np.float32
    bf = ml_dtypes.bfloat16
    A = np.ascontiguousarray
    g = {}
    g["wm"] = A(np.asarray(inputs["L0_ffn_Wm"], f32).reshape(NG, 128, D).transpose(1, 0, 2))
    g["wo"] = A(np.asarray(inputs["L0_ffn_Wo"], f32).reshape(NG, 128, D).transpose(1, 0, 2))
    g["w1"] = A(np.asarray(inputs["L0_ffn_W1"], f32).transpose(1, 0, 2))
    g["w2"] = A(np.asarray(inputs["L0_ffn_W2"], f32).reshape(NE, 4, 128, 128).transpose(2, 0, 1, 3))
    g["wq"] = A(np.asarray(inputs["L0_attn_Wq"], f32).reshape(NG, 128, DI).transpose(1, 0, 2))
    g["wk"] = A(np.asarray(inputs["L0_attn_Wk"], f32).reshape(NG, 128, DI).transpose(1, 0, 2))
    g["wv"] = A(np.asarray(inputs["L0_attn_Wv"], f32).reshape(NG, 128, DI).transpose(1, 0, 2))
    g["wa"] = A(np.asarray(inputs["L0_attn_Wa"], f32).reshape(NG, 128, DI).transpose(1, 0, 2))
    g["woa"] = A(np.asarray(inputs["L0_attn_Wo"], f32).reshape(4, 128, D).transpose(1, 0, 2))
    g["emb"] = A(np.asarray(inputs["L0_ffn_expert_emb"], f32))
    g["b1"] = A(np.asarray(inputs["L0_ffn_b1"], f32).reshape(NE, 4, 128).transpose(2, 0, 1))
    g["b2"] = A(np.asarray(inputs["L0_ffn_b2"], f32))
    g["bm"] = A(np.asarray(inputs["L0_ffn_bm"], f32).reshape(NG, 128).T)
    g["bo"] = A(np.asarray(inputs["L0_ffn_bo"], f32).reshape(NG, 128).T)
    g["ba"] = A(np.asarray(inputs["L0_attn_ba"], f32).reshape(4, 128).T)
    g["gamma"] = A(np.asarray(inputs["L0_attn_gamma"], f32).reshape(NG, 128).T)
    g["beta"] = A(np.asarray(inputs["L0_attn_beta"], f32).reshape(NG, 128).T)
    g["ident"] = np.eye(128, dtype=bf)
    g["ident32"] = np.eye(128, dtype=f32)
    g["ones1"] = np.ones((1, 128), dtype=bf)
    g["onesr"] = np.ones((1, 512), dtype=bf)
    g["onesc"] = np.ones((128, 1), dtype=bf)
    cols = np.zeros((128, 2), f32)
    cols[0:64, 0] = 1.0
    cols[64:128, 1] = 1.0
    g["cols01"] = cols.astype(bf)
    selm = np.zeros((NE, NE, 128), f32)
    for e in range(NE):
        selm[e, e, :] = 1.0
    g["sel"] = selm.reshape(NE, NE * 128).astype(bf)
    g["masku"] = np.triu(np.ones((CH, CH), f32)).astype(bf)
    g["epsv"] = np.full((1, 1), LN_EPS, f32)
    return g


def kernel(**inputs):
    nc = build_program()
    shared = _prep_shared(inputs)
    x = np.asarray(inputs["L0_attn_x"], np.float32)
    in_maps = []
    for c in range(8):
        m = dict(shared)
        m["xin"] = np.ascontiguousarray(
            x[c].T.reshape(NG, 128, T).transpose(1, 0, 2))
        in_maps.append(m)
    res = run_bass_kernel_spmd(nc, in_maps, core_ids=list(range(8)))
    outs = []
    for r in res.results:
        xo = np.asarray(r["xo"])
        outs.append(xo.transpose(1, 0, 2).reshape(D, T).T)
    return np.stack(outs, axis=0).astype(np.float32)
